# version 05
# speedup vs baseline: 1.3643x; 1.1313x over previous
import math
from contextlib import ExitStack

import numpy as np
import concourse.bass as bass
import concourse.mybir as mybir
from concourse.bass_utils import run_bass_kernel_spmd

F32 = mybir.dt.float32
BF16 = mybir.dt.bfloat16
F16 = mybir.dt.float16
AF = mybir.ActivationFunctionType
ALU = mybir.AluOpType
AX = mybir.AxisListType

NCORES = 8
L = 8192
D_MODEL = 1024
LOC = L // 4
D_INNER = 2048
HALO = 128
LOCH = LOC + HALO
EPS = 1e-5

_DRAIN_OPS = {
    "tensor_copy", "tensor_tensor", "tensor_add", "tensor_sub", "tensor_mul",
    "tensor_max", "tensor_scalar", "tensor_single_scalar", "tensor_scalar_add",
    "tensor_scalar_sub", "tensor_scalar_mul", "tensor_scalar_max",
    "tensor_scalar_min", "tensor_relu", "tensor_reduce", "reduce_max",
    "reduce_sum", "scalar_tensor_tensor", "tensor_tensor_scan", "memset",
    "memzero", "reciprocal", "select", "copy_predicated", "transpose",
    "activation", "copy", "mul", "add", "square", "sqrt", "sign", "iota",
    "pool", "pool_avg", "pool_max", "max", "max_index", "stream_shuffle",
}


class _AutoDrain:

    def __init__(self, eng):
        object.__setattr__(self, "_e", eng)

    def __getattr__(self, name):
        attr = getattr(self._e, name)
        if name in _DRAIN_OPS and callable(attr):
            def call(*a, _attr=attr, **kw):
                ins = _attr(*a, **kw)
                self._e.drain()
                return ins
            return call
        return attr


class Sem:
    __slots__ = ("h", "n")

    def __init__(self, h):
        self.h = h
        self.n = 0

    def inc(self, k=1):
        self.n += k
        return self.n


class DmaQ:

    def __init__(self, prog, name, depth=3):
        self.sems = [prog.sem(f"{name}{i}") for i in range(depth)]
        self.depth = depth
        self.i = 0

    def tick(self):
        j = self.i
        self.i += 1
        s = self.sems[j % self.depth]
        pre = 16 * (j // self.depth)
        s.n = pre + 16
        return s.h, pre, s.n

    def all_done(self):
        return [(s.h, s.n) for s in self.sems if s.n]


class Prog:
    ENGINES = ("sync", "gpsimd", "vector", "scalar", "tensor")

    def __init__(self, name):
        self.name = name
        self.nc = bass.Bass("TRN2", target_bir_lowering=False)
        self.ops = {e: [] for e in self.ENGINES}
        self.stack = ExitStack()
        self._semcnt = 0

    def sem(self, name=None):
        name = name or f"s{self._semcnt}"
        self._semcnt += 1
        h = self.stack.enter_context(self.nc.semaphore(name))
        return Sem(h)

    def emit(self, engine, fn):
        self.ops[engine].append(fn)

    def dma(self, q, engine, out, in_, waits=(), transpose=False):
        h, pre, done = q.tick()
        waits = tuple(waits)

        def thunk(eng, h=h, pre=pre, waits=waits, out=out, in_=in_, tr=transpose):
            for sh, v in waits:
                if v:
                    eng.wait_ge(sh, v)
            if pre:
                eng.wait_ge(h, pre)
            if tr:
                eng.dma_start_transpose(out, in_).then_inc(h, 16)
            else:
                eng.dma_start(out=out, in_=in_).then_inc(h, 16)
        self.emit(engine, thunk)
        return (h, done)

    def finish(self):
        with self.nc.Block() as block:
            for eng_name in self.ENGINES:
                thunks = self.ops[eng_name]
                if not thunks:
                    continue
                dec = getattr(block, eng_name)
                wrap = eng_name in ("vector", "scalar", "gpsimd")

                def body(eng, thunks=thunks, wrap=wrap):
                    if wrap:
                        eng = _AutoDrain(eng)
                    for f in thunks:
                        f(eng)
                dec(body)
        self.stack.close()
        return self.nc


def dram_in(nc, name, shape, dtype=F32):
    return nc.dram_tensor(name, list(shape), dtype, kind="ExternalInput").ap()


def dram_out(nc, name, shape, dtype=F32):
    return nc.dram_tensor(name, list(shape), dtype, kind="ExternalOutput").ap()


def bcast_ap(dram_ap, row_len, nrows=128, offset=0):
    return bass.AP(dram_ap.tensor, offset, [[0, nrows], [1, row_len]])


def W(eng, *waits):
    for w in waits:
        if w:
            sh, v = w
            if v:
                eng.wait_ge(sh, v)


def emit_ln_nonaffine(p, s_v, s_s, xb, seq_out, scr_sq, nm, ssq, rstd, cst,
                      pre_waits_v=(), d=D_MODEL):
    dst = xb if seq_out is None else seq_out

    def v1(eng, waits=tuple(pre_waits_v)):
        W(eng, *waits)
        eng.tensor_reduce(nm[:, 0:1], xb[:, :], AX.X, ALU.add, negate=True)
        eng.tensor_scalar(nm[:, 1:2], nm[:, 0:1], 1.0 / d, None, ALU.mult)
        eng.tensor_scalar(dst[:, :], xb[:, :], nm[:, 1:2], None, ALU.add
                          ).then_inc(s_v.h, 1)
    p.emit("vector", v1)
    v1n = s_v.inc()

    def s1(eng, v1n=v1n):
        W(eng, (s_v.h, v1n))
        eng.activation(scr_sq[:, :], dst[:, :], AF.Square, accum_out=ssq[:, 0:1])
        eng.activation(ssq[:, 1:2], ssq[:, 0:1], AF.Ln, bias=cst[:, 0:1], scale=1.0 / d)
        eng.activation(rstd[:, 0:1], ssq[:, 1:2], AF.Exp, scale=-0.5).then_inc(s_s.h, 1)
    p.emit("scalar", s1)
    s1n = s_s.inc()

    def v2(eng, s1n=s1n):
        W(eng, (s_s.h, s1n))
        eng.tensor_scalar(dst[:, :], dst[:, :], rstd[:, 0:1], None, ALU.mult
                          ).then_inc(s_v.h, 1)
    p.emit("vector", v2)
    v2n = s_v.inc()
    return v1n, s1n, v2n



def build_LA():
    p = Prog("LA")
    nc = p.nc

    xh_d = dram_in(nc, "xh", [LOCH, D_MODEL])
    lng_d = dram_in(nc, "ln_g", [D_MODEL])
    lnb_d = dram_in(nc, "ln_b", [D_MODEL])
    wip_d = dram_in(nc, "in_projT", [D_MODEL, 2 * D_INNER])
    wcv_d = dram_in(nc, "conv_wp", [128, 64])
    bcv_d = dram_in(nc, "conv_bp", [128, 16])
    wxp_d = dram_in(nc, "x_projT", [D_INNER, 96])
    wdt_d = dram_in(nc, "dt_projT", [64, D_INNER])
    bdt_d = dram_in(nc, "dt_proj_bp", [128, 16])
    ident_d = dram_in(nc, "ident", [128, 128])
    hmask_d = dram_in(nc, "halo_mask", [128, 3])

    xs_o = dram_out(nc, "xs", [D_INNER, LOC], F32)
    zs_o = dram_out(nc, "zs", [D_INNER, LOC], F32)
    dtT_o = dram_out(nc, "dtT", [D_INNER, LOC], F32)
    bc_o = dram_out(nc, "bc", [32, LOC], F32)

    NKIN = D_MODEL // 128
    NT = LOCH // 128

    a = nc.alloc_sbuf_tensor
    gb = a("gb", [128, D_MODEL], F32)
    bb = a("bb", [128, D_MODEL], F32)
    w_in = a("w_in", [128, NKIN * 2 * D_INNER], BF16)
    w_cv = a("w_cv", [128, 64], F32)
    b_cv = a("b_cv", [128, 16], F32)
    w_xp = a("w_xp", [128, 16 * 96], BF16)
    w_dt = a("w_dt", [64, D_INNER], BF16)
    b_dt = a("b_dt", [128, 16], F32)
    ident = a("ident_sb", [128, 128], BF16)
    hmask = a("hmask_sb", [128, 3], F32)
    cst = a("cst", [128, 2], F32)

    xt = [a(f"xt{i}", [128, D_MODEL], F32) for i in range(2)]
    scr_sq = a("scr_sq", [128, D_MODEL], F32)
    nm = a("nm", [128, 4], F32)
    ssq = a("ssq", [128, 4], F32)
    rstd = a("rstd", [128, 2], F32)
    xaff = a("xaff", [128, D_MODEL], F32)
    xn_bf = a("xn_bf", [128, D_MODEL], BF16)
    xnT = a("xnT", [128, NKIN * LOCH], BF16)

    xslab = [a(f"xslab{i}", [128, LOCH], F32) for i in range(2)]
    conv_t = a("conv_t", [128, LOC], F32)
    xs_bf = [a(f"xs_bf{i}", [128, LOC], BF16) for i in range(2)]
    zs_bf = [a(f"zs_bf{i}", [128, LOCH], BF16) for i in range(2)]
    xdbl = a("xdbl", [96, LOC], F32)
    xdbl_bf = a("xdbl_bf", [64, LOC], BF16)
    dt_slab = [a(f"dt_slab{i}", [128, LOC], F16) for i in range(2)]
    dt_r = a("dt_r", [128, 512], F32)
    dt_a = a("dt_a", [128, 512], F32)

    pa = nc.alloc_psum_tensor
    ptr = pa("ptr", [128, 1024], BF16)
    pmm = [pa(f"pmm{i}", [128, 512], F32) for i in range(2)]
    pxd = pa("pxd", [96, LOC], F32)

    qc = DmaQ(p, "qc", 2)
    qx = DmaQ(p, "qx", 2)
    qw = DmaQ(p, "qw", 3)
    qo = DmaQ(p, "qo", 3)
    qb = DmaQ(p, "qb", 1)
    s_v = p.sem("v")
    s_s = p.sem("s")
    s_t = p.sem("t")

    c1 = p.dma(qc, "sync", gb[:, :], bcast_ap(lng_d, D_MODEL))
    c2 = p.dma(qc, "sync", bb[:, :], bcast_ap(lnb_d, D_MODEL))
    c3 = p.dma(qc, "sync", hmask[:, :], hmask_d)
    c4 = p.dma(qc, "sync", w_cv[:, :], wcv_d)
    c5 = p.dma(qc, "sync", b_cv[:, :], bcv_d)
    c6 = p.dma(qc, "sync", b_dt[:, :], bdt_d)

    wl0 = p.dma(qw, "gpsimd", ident[:, :], ident_d)
    for s in range(NKIN):
        p.dma(qw, "gpsimd",
              w_in[:, s * 2 * D_INNER:(s + 1) * 2 * D_INNER],
              bass.AP(wip_d.tensor, s * 128 * 2 * D_INNER,
                      [[2 * D_INNER, 128], [1, 2 * D_INNER]]))
    p.dma(qw, "gpsimd", w_xp[:, :],
          bass.AP(wxp_d.tensor, 0, [[96, 128], [128 * 96, 16], [1, 96]]))
    p.dma(qw, "gpsimd", w_dt[:, :], wdt_d)
    weights_done = tuple(qw.all_done())

    def gcst(eng):
        eng.memset(cst[:, 0:1], EPS)
        eng.memset(cst[:, 1:2], 1.0).then_inc(s_v.h, 1)
    p.emit("vector", gcst)
    cst_done_v = s_v.inc()

    s1_of, v2_of, v4_of = [], [], []
    for i in range(NT):
        buf = xt[i % 2]
        lw = []
        if i >= 2:
            lw = [(s_v.h, v2_of[i - 2]), (s_s.h, s1_of[i - 2])]
        ldt = p.dma(qx, "sync", buf[:, :], xh_d[i * 128:(i + 1) * 128, :], waits=lw)

        def v1(eng, buf=buf, ldt=ldt, sfree=s_s.n):
            W(eng, ldt, (s_s.h, sfree))
            eng.tensor_reduce(nm[:, 0:1], buf[:, :], AX.X, ALU.add, negate=True)
            eng.tensor_scalar(nm[:, 1:2], nm[:, 0:1], 1.0 / D_MODEL, None, ALU.mult)
            eng.tensor_scalar(buf[:, :], buf[:, :], nm[:, 1:2], None, ALU.add
                              ).then_inc(s_v.h, 1)
        p.emit("vector", v1)
        v1n = s_v.inc()

        def s1(eng, buf=buf, v1n=v1n, cd=cst_done_v):
            W(eng, (s_v.h, max(v1n, cd)))
            eng.activation(scr_sq[:, :], buf[:, :], AF.Square, accum_out=ssq[:, 0:1])
            eng.activation(ssq[:, 1:2], ssq[:, 0:1], AF.Ln, bias=cst[:, 0:1],
                           scale=1.0 / D_MODEL)
            eng.activation(rstd[:, 0:1], ssq[:, 1:2], AF.Exp, scale=-0.5).then_inc(s_s.h, 1)
        p.emit("scalar", s1)
        s1_of.append(s_s.inc())

        def v2(eng, buf=buf, s1n=s1_of[-1]):
            W(eng, (s_s.h, s1n))
            eng.tensor_scalar(xaff[:, :], buf[:, :], rstd[:, 0:1], None, ALU.mult)
            eng.tensor_reduce(nm[:, 2:3], xaff[:, :], AX.X, ALU.add, negate=True)
            eng.tensor_scalar(nm[:, 3:4], nm[:, 2:3], 1.0 / D_MODEL, None, ALU.mult)
            eng.tensor_scalar(xaff[:, :], xaff[:, :], nm[:, 3:4], None, ALU.add
                              ).then_inc(s_v.h, 1)
        p.emit("vector", v2)
        v2_of.append(s_v.inc())

        def s2(eng, v2n=v2_of[-1]):
            W(eng, (s_v.h, v2n))
            eng.activation(scr_sq[:, :], xaff[:, :], AF.Square, accum_out=ssq[:, 2:3])
            eng.activation(ssq[:, 3:4], ssq[:, 2:3], AF.Ln, bias=cst[:, 0:1],
                           scale=1.0 / D_MODEL)
            eng.activation(rstd[:, 1:2], ssq[:, 3:4], AF.Exp, scale=-0.5).then_inc(s_s.h, 1)
        p.emit("scalar", s2)
        s2n = s_s.inc()

        def v3(eng, s2n=s2n, c1=c1, c2=c2, tprev=(s_t.n if i >= 1 else 0)):
            W(eng, (s_s.h, s2n), c1, c2)
            if tprev:
                eng.wait_ge(s_t.h, tprev)
            eng.scalar_tensor_tensor(xaff[:, :], xaff[:, :], rstd[:, 1:2], gb[:, :],
                                     ALU.mult, ALU.mult)
            eng.tensor_add(xn_bf[:, :], xaff[:, :], bb[:, :]).then_inc(s_v.h, 1)
        p.emit("vector", v3)
        v3n = s_v.inc()

        def t1(eng, v3n=v3n, wd=wl0, need_ptr=(v4_of[i - 1] if i >= 1 else 0)):
            W(eng, (s_v.h, v3n), wd)
            if need_ptr:
                eng.wait_ge(s_v.h, need_ptr)
            for j in range(NKIN):
                ins = eng.transpose(ptr[:, j * 128:(j + 1) * 128],
                                    xn_bf[:, j * 128:(j + 1) * 128], ident[:, :])
            ins.then_inc(s_t.h, 1)
        p.emit("tensor", t1)
        t1n = s_t.inc()

        def v4(eng, i=i, t1n=t1n):
            W(eng, (s_t.h, t1n))
            eng.tensor_copy(
                xnT[:, :].rearrange("p (s t) -> p s t", s=NKIN)[:, :, i * 128:(i + 1) * 128],
                ptr[:, :].rearrange("p (s t) -> p s t", s=NKIN),
            ).then_inc(s_v.h, 1)
        p.emit("vector", v4)
        v4_of.append(s_v.inc())
    ph1_v = s_v.n

    CHUNKS = [(0, 512), (512, 512), (1024, 512), (1536, 512), (2048, 128)]
    conv_done = {}
    zdma = {}
    xs_dma = {}
    xs_mm = {}
    bank_evac = {0: 0, 1: 0}
    for slab in range(32):
        is_x = slab < 16
        par = slab % 2
        xb, zb, xsb = xslab[par], zs_bf[par], xs_bf[par]

        for ci, (c0, cw) in enumerate(CHUNKS):
            pb = pmm[ci % 2]

            def tmmc(eng, slab=slab, c0=c0, cw=cw, pb=pb, ph1=ph1_v,
                     evac_need=bank_evac[ci % 2], wd=weights_done):
                W(eng, (s_v.h, ph1), *wd)
                if evac_need > 0:
                    eng.wait_ge(s_s.h, evac_need)
                for k in range(NKIN):
                    ins = eng.matmul(pb[:, 0:cw],
                                     w_in[:, k * 2 * D_INNER + slab * 128:
                                          k * 2 * D_INNER + (slab + 1) * 128],
                                     xnT[:, k * LOCH + c0:k * LOCH + c0 + cw],
                                     start=(k == 0), stop=(k == NKIN - 1))
                ins.then_inc(s_t.h, 1)
            p.emit("tensor", tmmc)
            mmn = s_t.inc()

            def sev(eng, is_x=is_x, c0=c0, cw=cw, pb=pb, mmn=mmn, xb=xb, zb=zb,
                    vfree=conv_done.get(par, 0), ofree=zdma.get(par)):
                W(eng, (s_t.h, mmn))
                if is_x and vfree:
                    eng.wait_ge(s_v.h, vfree)
                if (not is_x) and ofree:
                    W(eng, ofree)
                if is_x:
                    eng.copy(xb[:, c0:c0 + cw], pb[:, 0:cw]).then_inc(s_s.h, 1)
                else:
                    eng.activation(zb[:, c0:c0 + cw], pb[:, 0:cw], AF.Silu
                                   ).then_inc(s_s.h, 1)
            p.emit("scalar", sev)
            bank_evac[ci % 2] = s_s.inc()
        slab_evac = s_s.n

        if is_x:
            def vcv(eng, slab=slab, xb=xb, need=slab_evac, c3=c3, c4=c4, c5=c5):
                W(eng, (s_s.h, need), c3, c4, c5)
                eng.tensor_mul(xb[:, HALO - 3:HALO], xb[:, HALO - 3:HALO], hmask[:, :])
                eng.tensor_scalar(conv_t[:, :], xb[:, HALO - 3:HALO - 3 + LOC],
                                  w_cv[:, slab * 4:slab * 4 + 1], None, ALU.mult)
                for k in range(1, 4):
                    eng.scalar_tensor_tensor(conv_t[:, :],
                                             xb[:, HALO - 3 + k:HALO - 3 + k + LOC],
                                             w_cv[:, slab * 4 + k:slab * 4 + k + 1],
                                             conv_t[:, :], ALU.mult, ALU.add)
                eng.tensor_scalar(conv_t[:, :], conv_t[:, :], b_cv[:, slab:slab + 1],
                                  None, ALU.add).then_inc(s_v.h, 1)
            p.emit("vector", vcv)
            conv_done[par] = s_v.inc()

            def ssl(eng, xsb=xsb, vcvn=conv_done[par],
                    ofree=xs_dma.get(par), tfree=xs_mm.get(par, 0)):
                W(eng, (s_v.h, vcvn))
                if ofree:
                    W(eng, ofree)
                if tfree:
                    eng.wait_ge(s_t.h, tfree)
                eng.activation(xsb[:, :], conv_t[:, :], AF.Silu).then_inc(s_s.h, 1)
            p.emit("scalar", ssl)
            ssln = s_s.inc()

            xs_dma[par] = p.dma(qo, "gpsimd", xs_o[slab * 128:(slab + 1) * 128, :],
                                xsb[:, :], waits=[(s_s.h, ssln)])

            def txp(eng, slab=slab, xsb=xsb, ssln=ssln):
                W(eng, (s_s.h, ssln))
                for ci4 in range(4):
                    ins = eng.matmul(pxd[:, ci4 * 512:(ci4 + 1) * 512],
                                     w_xp[:, slab * 96:(slab + 1) * 96],
                                     xsb[:, ci4 * 512:(ci4 + 1) * 512],
                                     start=(slab == 0), stop=(slab == 15))
                ins.then_inc(s_t.h, 1)
            p.emit("tensor", txp)
            xs_mm[par] = s_t.inc()
        else:
            zdma[par] = p.dma(qo, "gpsimd", zs_o[(slab - 16) * 128:(slab - 15) * 128, :],
                              zb[:, HALO:], waits=[(s_s.h, slab_evac)])
    ph2_t = s_t.n

    def vxd(eng, need=ph2_t):
        W(eng, (s_t.h, need))
        eng.tensor_copy(xdbl[:, :], pxd[:, :]).then_inc(s_v.h, 1)
    p.emit("vector", vxd)
    vxdn = s_v.inc()

    bc_done = p.dma(qb, "sync", bc_o[:, :], xdbl[64:96, :], waits=[(s_v.h, vxdn)])

    def sxc(eng, vxdn=vxdn):
        W(eng, (s_v.h, vxdn))
        eng.copy(xdbl_bf[:, :], xdbl[0:64, :]).then_inc(s_s.h, 1)
    p.emit("scalar", sxc)
    sxcn = s_s.inc()

    dt_dma = {}
    vdt_of = []
    dt_bank_evac = dict(bank_evac)
    for dchunk in range(16):
        dsl = dt_slab[dchunk % 2]
        for ci in range(4):
            pb = pmm[ci % 2]

            def tdt(eng, dchunk=dchunk, pb=pb, ci=ci, sxcn=sxcn,
                    evac_need=dt_bank_evac[ci % 2]):
                eng.wait_ge(s_s.h, max(sxcn, evac_need))
                eng.matmul(pb[:, :], w_dt[:, dchunk * 128:(dchunk + 1) * 128],
                           xdbl_bf[:, ci * 512:(ci + 1) * 512], start=True, stop=True
                           ).then_inc(s_t.h, 1)
            p.emit("tensor", tdt)
            mmn = s_t.inc()

            def sdt(eng, dchunk=dchunk, pb=pb, mmn=mmn, cd=cst_done_v, c6=c6,
                    vfree=(vdt_of[-1] if vdt_of else 0)):
                W(eng, (s_t.h, mmn), c6)
                eng.wait_ge(s_v.h, max(cd, vfree))
                eng.activation(dt_a[:, :], pb[:, :], AF.Abs, bias=b_dt[:, dchunk:dchunk + 1])
                eng.activation(dt_a[:, :], dt_a[:, :], AF.Exp, scale=-1.0)
                eng.activation(dt_a[:, :], dt_a[:, :], AF.Ln, bias=cst[:, 1:2])
                eng.activation(dt_r[:, :], pb[:, :], AF.Relu,
                               bias=b_dt[:, dchunk:dchunk + 1]).then_inc(s_s.h, 1)
            p.emit("scalar", sdt)
            sdtn = s_s.inc()
            dt_bank_evac[ci % 2] = sdtn

            def vdt(eng, ci=ci, dsl=dsl, sdtn=sdtn,
                    ofree=(dt_dma.get(dchunk % 2) if ci == 0 else None)):
                W(eng, (s_s.h, sdtn))
                if ofree:
                    W(eng, ofree)
                eng.tensor_add(dsl[:, ci * 512:(ci + 1) * 512], dt_r[:, :], dt_a[:, :]
                               ).then_inc(s_v.h, 1)
            p.emit("vector", vdt)
            vdt_of.append(s_v.inc())

        dt_dma[dchunk % 2] = p.dma(qo, "gpsimd",
                                   dtT_o[dchunk * 128:(dchunk + 1) * 128, :],
                                   dsl[:, :], waits=[(s_v.h, vdt_of[-1])])

    outs_done = tuple(qo.all_done()) + (bc_done,)

    def fin(eng, outs=outs_done):
        W(eng, *outs)
    p.emit("sync", fin)
    return p.finish()



def build_LC():
    p = Prog("LC")
    nc = p.nc
    DSL = 512
    NSL = DSL // 128
    TCH = 2048
    NCH = L // TCH

    xs_d = dram_in(nc, "xs", [DSL, L])
    dt_d = dram_in(nc, "dtT", [DSL, L])
    zs_d = dram_in(nc, "zs", [DSL, L])
    bc_d = dram_in(nc, "bc", [32, L])
    al_d = dram_in(nc, "A_logp", [128, NSL * 16])
    dd_d = dram_in(nc, "Dp", [128, NSL])

    yg_o = dram_out(nc, "ygT", [DSL, L], F32)

    a = nc.alloc_sbuf_tensor
    Acoef = a("Acoef", [128, NSL * 16], F32)
    Dcoef = a("Dcoef", [128, NSL], F32)
    dts = a("dts", [128, NSL * TCH], F16)
    us = a("us", [128, NSL * TCH], F16)
    xst = a("xst", [128, TCH], F32)
    zst = a("zst", [128, TCH], F32)
    Bb = [a(f"Bb{i}", [128, TCH], F32) for i in range(2)]
    Cb = [a(f"Cb{i}", [128, TCH], F32) for i in range(2)]
    abuf = a("abuf", [128, TCH], F32)
    bbuf = a("bbuf", [128, TCH], BF16)
    hbuf = a("hbuf", [128, TCH], F16)
    tbuf = a("tbuf", [128, TCH], F32)
    ybuf = a("ybuf", [128, NSL * TCH], F32)
    hlast = a("hlast", [128, NSL * 16], F32)

    qa = DmaQ(p, "qa", 2)
    qd = DmaQ(p, "qd", 2)
    qbc = DmaQ(p, "qbc", 2)
    qo = DmaQ(p, "qo", 2)
    s_v = p.sem("v")
    s_s = p.sem("s")

    cA = p.dma(qa, "sync", Acoef[:, :], al_d)
    cD = p.dma(qa, "sync", Dcoef[:, :], dd_d)

    def sA(eng, cA=cA):
        W(eng, cA)
        eng.activation(Acoef[:, :], Acoef[:, :], AF.Exp)
        eng.activation(Acoef[:, :], Acoef[:, :], AF.Copy, scale=-1.0).then_inc(s_s.h, 1)
    p.emit("scalar", sA)
    sA_n = s_s.inc()

    def vz(eng):
        eng.memset(hlast[:, :], 0.0).then_inc(s_v.h, 1)
    p.emit("vector", vz)
    s_v.inc()

    for ch in range(NCH):
        t0 = ch * TCH
        for sl in range(NSL):
            dw = []
            if ch > 0 and sl == 0:
                dw = [(s_v.h, s_v.n), (s_s.h, s_s.n)]
            dtt = p.dma(qd, "gpsimd", dts[:, sl * TCH:(sl + 1) * TCH],
                        dt_d[sl * 128:(sl + 1) * 128, t0:t0 + TCH], waits=dw)
            xst_t = p.dma(qa, "sync", xst[:, :],
                          xs_d[sl * 128:(sl + 1) * 128, t0:t0 + TCH],
                          waits=[(s_v.h, s_v.n)])

            def vu(eng, sl=sl, dtt=dtt, xst_t=xst_t):
                W(eng, dtt, xst_t)
                eng.tensor_mul(us[:, sl * TCH:(sl + 1) * TCH],
                               dts[:, sl * TCH:(sl + 1) * TCH], xst[:, :]
                               ).then_inc(s_v.h, 1)
            p.emit("vector", vu)
            s_v.inc()
        u_done = s_v.n

        def vy0(eng, need=u_done, oneed=tuple(qo.all_done())):
            W(eng, (s_v.h, need), *oneed)
            eng.memset(ybuf[:, :], 0.0).then_inc(s_v.h, 1)
        p.emit("vector", vy0)
        s_v.inc()

        for s in range(16):
            bbp, cbp = Bb[s % 2], Cb[s % 2]
            bw = [(s_v.h, s_v.n - NSL)] if s >= 2 else []
            bt = p.dma(qbc, "gpsimd", bbp[:, :],
                       bass.AP(bc_d.tensor, s * L + t0, [[0, 128], [1, TCH]]), waits=bw)
            ct = p.dma(qbc, "gpsimd", cbp[:, :],
                       bass.AP(bc_d.tensor, (16 + s) * L + t0, [[0, 128], [1, TCH]]))

            for sl in range(NSL):
                def sa_(eng, s=s, sl=sl, sA_n=sA_n, u_done=u_done, vprev=s_v.n):
                    W(eng, (s_s.h, sA_n), (s_v.h, max(u_done, vprev)))
                    eng.activation(abuf[:, :], dts[:, sl * TCH:(sl + 1) * TCH],
                                   AF.Exp, scale=Acoef[:, sl * 16 + s:sl * 16 + s + 1]
                                   ).then_inc(s_s.h, 1)
                p.emit("scalar", sa_)
                san = s_s.inc()

                def vscan(eng, s=s, sl=sl, san=san, bt=bt, ct=ct, bbp=bbp, cbp=cbp):
                    W(eng, (s_s.h, san), bt, ct)
                    eng.tensor_mul(bbuf[:, :], us[:, sl * TCH:(sl + 1) * TCH], bbp[:, :])
                    eng.tensor_tensor_scan(hbuf[:, :], abuf[:, :], bbuf[:, :],
                                           hlast[:, sl * 16 + s:sl * 16 + s + 1],
                                           ALU.mult, ALU.add)
                    eng.tensor_copy(hlast[:, sl * 16 + s:sl * 16 + s + 1],
                                    hbuf[:, TCH - 1:TCH])
                    eng.tensor_mul(tbuf[:, :], hbuf[:, :], cbp[:, :])
                    eng.tensor_add(ybuf[:, sl * TCH:(sl + 1) * TCH],
                                   ybuf[:, sl * TCH:(sl + 1) * TCH], tbuf[:, :]
                                   ).then_inc(s_v.h, 1)
                p.emit("vector", vscan)
                s_v.inc()

        for sl in range(NSL):
            xst_t = p.dma(qa, "sync", xst[:, :],
                          xs_d[sl * 128:(sl + 1) * 128, t0:t0 + TCH],
                          waits=[(s_v.h, s_v.n)])
            zst_t = p.dma(qa, "sync", zst[:, :],
                          zs_d[sl * 128:(sl + 1) * 128, t0:t0 + TCH])

            def vg(eng, sl=sl, xst_t=xst_t, zst_t=zst_t, cD=cD):
                W(eng, xst_t, zst_t, cD)
                eng.scalar_tensor_tensor(ybuf[:, sl * TCH:(sl + 1) * TCH],
                                         xst[:, :], Dcoef[:, sl:sl + 1],
                                         ybuf[:, sl * TCH:(sl + 1) * TCH],
                                         ALU.mult, ALU.add)
                eng.tensor_mul(ybuf[:, sl * TCH:(sl + 1) * TCH],
                               ybuf[:, sl * TCH:(sl + 1) * TCH], zst[:, :]
                               ).then_inc(s_v.h, 1)
            p.emit("vector", vg)
            vgn = s_v.inc()

            p.dma(qo, "sync", yg_o[sl * 128:(sl + 1) * 128, t0:t0 + TCH],
                  ybuf[:, sl * TCH:(sl + 1) * TCH], waits=[(s_v.h, vgn)])

    def fin(eng, outs=tuple(qo.all_done())):
        W(eng, *outs)
    p.emit("sync", fin)
    return p.finish()



def build_LD():
    p = Prog("LD")
    nc = p.nc
    NT = LOC // 128

    yg_d = dram_in(nc, "ygT", [D_INNER, LOC])
    x_d = dram_in(nc, "x", [LOC, D_MODEL])
    wo_d = dram_in(nc, "out_projT", [D_INNER, D_MODEL])
    rs_d = dram_in(nc, "res_scale", [1, 1])

    x1_o = dram_out(nc, "x1", [LOC, D_MODEL], F32)

    a = nc.alloc_sbuf_tensor
    w_o = a("w_o", [128, 16 * D_MODEL], BF16)
    ygt = [a(f"ygt{i}", [128, 16 * 128], BF16) for i in range(2)]
    xt = [a(f"xt{i}", [128, D_MODEL], F32) for i in range(2)]
    seqb = a("seqb", [128, D_MODEL], F32)
    resid = a("resid", [128, D_MODEL], F32)
    scr_sq = a("scr_sq", [128, D_MODEL], F32)
    nm = a("nm", [128, 2], F32)
    ssq = a("ssq", [128, 2], F32)
    rstd = a("rstd", [128, 1], F32)
    rsb = a("rsb", [128, 1], F32)
    cst = a("cst", [128, 1], F32)
    outb = [a(f"outb{i}", [128, D_MODEL], F32) for i in range(2)]

    pmm = [nc.alloc_psum_tensor(f"pmm{i}", [128, 512], F32) for i in range(2)]

    qc = DmaQ(p, "qc", 1)
    qx = DmaQ(p, "qx", 2)
    qy = DmaQ(p, "qy", 2)
    qw = DmaQ(p, "qw", 2)
    qo = DmaQ(p, "qo", 2)
    s_v = p.sem("v")
    s_s = p.sem("s")
    s_t = p.sem("t")

    rs_done = p.dma(qc, "sync", rsb[:, :], bcast_ap(rs_d, 1))

    def gcst(eng):
        eng.memset(cst[:, 0:1], EPS).then_inc(s_v.h, 1)
    p.emit("vector", gcst)
    cst_done_v = s_v.inc()

    for k in range(16):
        p.dma(qw, "gpsimd", w_o[:, k * D_MODEL:(k + 1) * D_MODEL],
              wo_d[k * 128:(k + 1) * 128, :])
    w_all = tuple(qw.all_done())

    v2_of, tmm_of, out_of = [], [], []
    for i in range(NT):
        xb, yb, ob = xt[i % 2], ygt[i % 2], outb[i % 2]

        xw = [(s_v.h, v2_of[i - 2])] if i >= 2 else []
        ldx = p.dma(qx, "sync", xb[:, :], x_d[i * 128:(i + 1) * 128, :], waits=xw)
        yw = [(s_t.h, tmm_of[i - 2])] if i >= 2 else []
        ldy = p.dma(qy, "gpsimd", yb[:, :],
                    bass.AP(yg_d.tensor, i * 128, [[LOC, 128], [128 * LOC, 16], [1, 128]]),
                    waits=yw)

        def v1(eng, xb=xb, ldx=ldx, sprev=s_s.n):
            W(eng, ldx, (s_s.h, sprev))
            eng.tensor_reduce(nm[:, 0:1], xb[:, :], AX.X, ALU.add, negate=True)
            eng.tensor_scalar(nm[:, 1:2], nm[:, 0:1], 1.0 / D_MODEL, None, ALU.mult)
            eng.tensor_scalar(seqb[:, :], xb[:, :], nm[:, 1:2], None, ALU.add
                              ).then_inc(s_v.h, 1)
        p.emit("vector", v1)
        v1n = s_v.inc()

        def s1(eng, v1n=v1n, cd=cst_done_v):
            W(eng, (s_v.h, max(v1n, cd)))
            eng.activation(scr_sq[:, :], seqb[:, :], AF.Square, accum_out=ssq[:, 0:1])
            eng.activation(ssq[:, 1:2], ssq[:, 0:1], AF.Ln, bias=cst[:, 0:1],
                           scale=1.0 / D_MODEL)
            eng.activation(rstd[:, 0:1], ssq[:, 1:2], AF.Exp, scale=-0.5).then_inc(s_s.h, 1)
        p.emit("scalar", s1)
        s1n = s_s.inc()

        def v2(eng, xb=xb, s1n=s1n):
            W(eng, (s_s.h, s1n))
            eng.scalar_tensor_tensor(resid[:, :], seqb[:, :], rstd[:, 0:1], xb[:, :],
                                     ALU.mult, ALU.add).then_inc(s_v.h, 1)
        p.emit("vector", v2)
        v2_of.append(s_v.inc())

        def tmm(eng, yb=yb, ldy=ldy, w_all=w_all, evac=(s_v.n if i >= 1 else 0)):
            W(eng, ldy, *w_all)
            if evac:
                eng.wait_ge(s_v.h, evac)
            for half in range(2):
                for k in range(16):
                    ins = eng.matmul(pmm[half][:, :],
                                     yb[:, k * 128:(k + 1) * 128],
                                     w_o[:, k * D_MODEL + half * 512:k * D_MODEL + half * 512 + 512],
                                     start=(k == 0), stop=(k == 15))
            ins.then_inc(s_t.h, 1)
        p.emit("tensor", tmm)
        tmm_of.append(s_t.inc())

        def v3(eng, ob=ob, tmmn=tmm_of[-1], v2n=v2_of[-1], rs=rs_done,
               ofree=(out_of[i - 2] if i >= 2 else None)):
            W(eng, (s_t.h, tmmn), rs)
            if ofree:
                W(eng, ofree)
            for half in range(2):
                ins = eng.scalar_tensor_tensor(ob[:, half * 512:half * 512 + 512],
                                               pmm[half][:, :], rsb[:, 0:1],
                                               resid[:, half * 512:half * 512 + 512],
                                               ALU.mult, ALU.add)
            ins.then_inc(s_v.h, 1)
        p.emit("vector", v3)
        v3n = s_v.inc()

        out_of.append(p.dma(qo, "sync", x1_o[i * 128:(i + 1) * 128, :], ob[:, :],
                            waits=[(s_v.h, v3n)]))

    def fin(eng, outs=tuple(qo.all_done())):
        W(eng, *outs)
    p.emit("sync", fin)
    return p.finish()



def _shard_tokens(x_full):
    shards = []
    for c in range(NCORES):
        b, ch = divmod(c, 4)
        lo = ch * LOC
        pad = np.zeros((HALO, x_full.shape[2]), dtype=x_full.dtype)
        if lo >= 3:
            pad[HALO - 3:] = x_full[b, lo - 3:lo]
        shards.append(np.ascontiguousarray(
            np.concatenate([pad, x_full[b, lo:lo + LOC]], axis=0)))
    return shards


def run_LA(inputs):
    x = np.asarray(inputs["L0_attn_seq"], dtype=np.float32)
    shards = _shard_tokens(x)
    cw = np.asarray(inputs["L0_attn_conv_w"], np.float32)
    cb = np.asarray(inputs["L0_attn_conv_b"], np.float32)
    db = np.asarray(inputs["L0_attn_dt_proj_b"], np.float32)
    common = dict(
        ln_g=np.asarray(inputs["L0_attn_ln_g"], np.float32),
        ln_b=np.asarray(inputs["L0_attn_ln_b"], np.float32),
        in_projT=np.ascontiguousarray(np.asarray(inputs["L0_attn_in_proj_w"], np.float32).T),
        conv_wp=np.ascontiguousarray(cw.reshape(16, 128, 4).transpose(1, 0, 2).reshape(128, 64)),
        conv_bp=np.ascontiguousarray(cb.reshape(16, 128).T),
        x_projT=np.ascontiguousarray(np.asarray(inputs["L0_attn_x_proj_w"], np.float32).T),
        dt_projT=np.ascontiguousarray(np.asarray(inputs["L0_attn_dt_proj_w"], np.float32).T),
        dt_proj_bp=np.ascontiguousarray(db.reshape(16, 128).T),
        ident=np.eye(128, dtype=np.float32),
    )
    in_maps = []
    for c in range(NCORES):
        hm = np.ones((128, 3), dtype=np.float32)
        if c % 4 == 0:
            hm[:, :] = 0.0
        m = dict(common)
        m["xh"] = shards[c]
        m["halo_mask"] = hm
        in_maps.append(m)
    nc = build_LA()
    res = run_bass_kernel_spmd(nc, in_maps, core_ids=list(range(NCORES)))
    return res.results


def run_LC_LD(inputs, la_results):
    xs_full = [np.concatenate([la_results[b * 4 + c]["xs"] for c in range(4)], axis=1)
               for b in range(2)]
    dt_full = [np.concatenate([la_results[b * 4 + c]["dtT"] for c in range(4)], axis=1)
               for b in range(2)]
    zs_full = [np.concatenate([la_results[b * 4 + c]["zs"] for c in range(4)], axis=1)
               for b in range(2)]
    bc_full = [np.concatenate([la_results[b * 4 + c]["bc"] for c in range(4)], axis=1)
               for b in range(2)]
    A_log = np.asarray(inputs["L0_attn_A_log"], np.float32)
    Dv = np.asarray(inputs["L0_attn_D"], np.float32)

    in_maps = []
    for c in range(NCORES):
        b, dsl = divmod(c, 4)
        r0 = dsl * 512
        al = A_log[r0:r0 + 512].reshape(4, 128, 16).transpose(1, 0, 2).reshape(128, 64)
        dp = Dv[r0:r0 + 512].reshape(4, 128).T
        in_maps.append(dict(
            xs=np.ascontiguousarray(xs_full[b][r0:r0 + 512]),
            dtT=np.ascontiguousarray(dt_full[b][r0:r0 + 512]),
            zs=np.ascontiguousarray(zs_full[b][r0:r0 + 512]),
            bc=np.ascontiguousarray(bc_full[b]),
            A_logp=np.ascontiguousarray(al),
            Dp=np.ascontiguousarray(dp),
        ))
    nc = build_LC()
    res_c = run_bass_kernel_spmd(nc, in_maps, core_ids=list(range(NCORES)))

    yg_full = [np.concatenate([res_c.results[b * 4 + d]["ygT"] for d in range(4)], axis=0)
               for b in range(2)]
    x = np.asarray(inputs["L0_attn_seq"], np.float32)
    out_projT = np.ascontiguousarray(np.asarray(inputs["L0_attn_out_proj_w"], np.float32).T)
    rs = np.asarray(inputs["L0_attn_res_scale"], np.float32).reshape(1, 1)
    in_maps = []
    for c in range(NCORES):
        b, ch = divmod(c, 4)
        t0 = ch * LOC
        in_maps.append(dict(
            ygT=np.ascontiguousarray(yg_full[b][:, t0:t0 + LOC]),
            x=np.ascontiguousarray(x[b, t0:t0 + LOC]),
            out_projT=out_projT,
            res_scale=rs,
        ))
    nc = build_LD()
    res_d = run_bass_kernel_spmd(nc, in_maps, core_ids=list(range(NCORES)))
    return res_c.results, res_d.results


def kernel(**inputs):
    raise NotImplementedError("under construction")



def build_LE():
    p = Prog("LE")
    nc = p.nc
    NT = LOC // 128
    NG = 4
    NK = D_MODEL // 128
    NH = 2048 // 128
    E = 8

    x1_d = dram_in(nc, "x1", [LOC, D_MODEL])
    gw_d = dram_in(nc, "gate_W", [D_MODEL, E])
    gb_d = dram_in(nc, "gate_b", [E])
    w1_d = dram_in(nc, "W1", [E, D_MODEL, 2048])
    b1_d = dram_in(nc, "b1r", [128, E, NH])
    w2_d = dram_in(nc, "W2", [E, 2048, D_MODEL])
    b2_d = dram_in(nc, "b2", [E, D_MODEL])
    gam_d = dram_in(nc, "ln_gamma", [D_MODEL])
    bet_d = dram_in(nc, "ln_beta", [D_MODEL])
    ident_d = dram_in(nc, "ident", [128, 128])
    ones_d = dram_in(nc, "ones1", [1, 128])

    x2_o = dram_out(nc, "x2", [LOC, D_MODEL], F32)
    xb_s = nc.dram_tensor("xb_scratch", [LOC, D_MODEL], F32, kind="Internal").ap()

    a = nc.alloc_sbuf_tensor
    gamb = a("gamb", [128, D_MODEL], F32)
    betb = a("betb", [128, D_MODEL], F32)
    gwt = a("gwt", [128, NK * E], BF16)
    gbt = a("gbt", [128, E], F32)
    onesb = a("onesb", [1, 128], BF16)
    ident = a("ident_sb", [128, 128], BF16)
    b1t = a("b1t", [128, E * NH], F32)
    b2t = a("b2t", [1, E * D_MODEL], BF16)
    w1t = a("w1t", [128, NK * 2048], BF16)
    w2t = a("w2t", [128, NH * D_MODEL], BF16)
    actT = a("actT", [128, NK * LOC], BF16)
    eo1 = a("eo1", [128, NT * D_MODEL], BF16)
    h1T = a("h1T", [128, NH * 512], BF16)
    xtile = [a(f"xtile{i}", [128, D_MODEL], F32) for i in range(2)]
    xnb = a("xnb", [128, D_MODEL], BF16)
    scr_sq = a("scr_sq", [128, D_MODEL], F32)
    nm = a("nm", [128, 2], F32)
    ssq = a("ssq", [128, 2], F32)
    rstd = a("rstd", [128, 1], F32)
    gwf = a("gwf", [128, E], F32)
    gmax = a("gmax", [128, 2], F32)
    zsum = a("zsum", [128, 2], F32)
    oh1 = a("oh1", [128, NT * E], F32)
    ohv2 = a("ohv2", [128, NT * E], F32)
    v1a = a("v1a", [128, NT], F32)
    scr8 = a("scr8", [128, E], F32)
    cst = a("cst", [128, 1], F32)
    outb = [a(f"outb{i}", [128, D_MODEL], F32) for i in range(2)]

    pa = nc.alloc_psum_tensor
    ptr = pa("ptr", [128, 1024], BF16)
    pg = pa("pg", [128, 8], F32)
    pf1 = [pa(f"pf1_{i}", [128, 512], F32) for i in range(2)]
    pf2 = [pa(f"pf2_{i}", [128, 512], F32) for i in range(2)]

    qc = DmaQ(p, "qc", 2)
    qx = DmaQ(p, "qx", 2)
    qw = DmaQ(p, "qw", 4)
    qs = DmaQ(p, "qs", 2)
    qo = DmaQ(p, "qo", 2)
    s_v = p.sem("v")
    s_s = p.sem("s")
    s_t = p.sem("t")

    c1 = p.dma(qc, "sync", gamb[:, :], bcast_ap(gam_d, D_MODEL))
    c2 = p.dma(qc, "sync", betb[:, :], bcast_ap(bet_d, D_MODEL))
    c3 = p.dma(qc, "sync", gbt[:, :], bcast_ap(gb_d, E))
    c4 = p.dma(qc, "sync", b1t[:, :], b1_d[:, :, :])

    g1 = p.dma(qw, "gpsimd", ident[:, :], ident_d)
    g2 = p.dma(qw, "gpsimd", onesb[:, :], ones_d)
    g3 = p.dma(qw, "gpsimd", gwt[:, :],
               bass.AP(gw_d.tensor, 0, [[E, 128], [128 * E, NK], [1, E]]))
    g4 = p.dma(qw, "gpsimd", b2t[:, :],
               bass.AP(b2_d.tensor, 0, [[0, 1], [1, E * D_MODEL]]))

    def gcst(eng):
        eng.memset(cst[:, 0:1], EPS).then_inc(s_v.h, 1)
    p.emit("vector", gcst)
    cst_done_v = s_v.inc()

    s1_of, v2_of, v4_of = [], [], []
    xbw_of = []
    for i in range(NT):
        xb = xtile[i % 2]
        lw = []
        if i >= 2:
            lw = [(s_v.h, v2_of[i - 2]), (s_s.h, s1_of[i - 2]), xbw_of[i - 2]]
        ldx = p.dma(qx, "sync", xb[:, :], x1_d[i * 128:(i + 1) * 128, :], waits=lw)

        def v1(eng, xb=xb, ldx=ldx, sfree=s_s.n, cd=cst_done_v):
            W(eng, ldx, (s_s.h, sfree))
            eng.wait_ge(s_v.h, cd)
            eng.tensor_reduce(nm[:, 0:1], xb[:, :], AX.X, ALU.add, negate=True)
            eng.tensor_scalar(nm[:, 1:2], nm[:, 0:1], 1.0 / D_MODEL, None, ALU.mult)
            eng.tensor_scalar(xb[:, :], xb[:, :], nm[:, 1:2], None, ALU.add
                              ).then_inc(s_v.h, 1)
        p.emit("vector", v1)
        v1n = s_v.inc()

        def s1(eng, xb=xb, v1n=v1n):
            W(eng, (s_v.h, v1n))
            eng.activation(scr_sq[:, :], xb[:, :], AF.Square, accum_out=ssq[:, 0:1])
            eng.activation(ssq[:, 1:2], ssq[:, 0:1], AF.Ln, bias=cst[:, 0:1],
                           scale=1.0 / D_MODEL)
            eng.activation(rstd[:, 0:1], ssq[:, 1:2], AF.Exp, scale=-0.5).then_inc(s_s.h, 1)
        p.emit("scalar", s1)
        s1_of.append(s_s.inc())

        def v2(eng, xb=xb, s1n=s1_of[-1], tfree=(s_t.n if i >= 1 else 0)):
            W(eng, (s_s.h, s1n))
            if tfree:
                eng.wait_ge(s_t.h, tfree)
            eng.tensor_scalar(xb[:, :], xb[:, :], rstd[:, 0:1], None, ALU.mult)
            eng.tensor_copy(xnb[:, :], xb[:, :]).then_inc(s_v.h, 1)
        p.emit("vector", v2)
        v2_of.append(s_v.inc())

        xbw_of.append(p.dma(qs, "sync", xb_s[i * 128:(i + 1) * 128, :], xb[:, :],
                            waits=[(s_v.h, v2_of[-1])]))

        def t1(eng, v2n=v2_of[-1], g1=g1, need_ptr=(v4_of[i - 1] if i >= 1 else 0)):
            W(eng, (s_v.h, v2n), g1)
            if need_ptr:
                eng.wait_ge(s_v.h, need_ptr)
            for j in range(NK):
                ins = eng.transpose(ptr[:, j * 128:(j + 1) * 128],
                                    xnb[:, j * 128:(j + 1) * 128], ident[:, :])
            ins.then_inc(s_t.h, 1)
        p.emit("tensor", t1)
        t1n = s_t.inc()

        def v4(eng, i=i, t1n=t1n):
            W(eng, (s_t.h, t1n))
            eng.tensor_copy(
                actT[:, :].rearrange("p (s t) -> p s t", s=NK)[:, :, i * 128:(i + 1) * 128],
                ptr[:, :].rearrange("p (s t) -> p s t", s=NK),
            ).then_inc(s_v.h, 1)
        p.emit("vector", v4)
        v4_of.append(s_v.inc())
    p1_v = s_v.n
    p1_xb = tuple(qs.all_done())

    pg_evac = 0
    for i in range(NT):
        def tgate(eng, i=i, p1=p1_v, g3=g3, vfree=pg_evac):
            W(eng, (s_v.h, p1), g3)
            if vfree:
                eng.wait_ge(s_v.h, vfree)
            for k in range(NK):
                ins = eng.matmul(pg[:, :],
                                 actT[:, k * LOC + i * 128:k * LOC + (i + 1) * 128],
                                 gwt[:, k * E:(k + 1) * E],
                                 start=(k == 0), stop=(k == NK - 1))
            ins.then_inc(s_t.h, 1)
        p.emit("tensor", tgate)
        tgn = s_t.inc()

        def vgate1(eng, tgn=tgn, c3=c3):
            W(eng, (s_t.h, tgn), c3)
            eng.tensor_add(gwf[:, :], pg[:, :], gbt[:, :])
            eng.tensor_reduce(gmax[:, 0:1], gwf[:, :], AX.X, ALU.max, negate=True
                              ).then_inc(s_v.h, 1)
        p.emit("vector", vgate1)
        vg1 = s_v.inc()

        def sgate(eng, vg1=vg1):
            W(eng, (s_v.h, vg1))
            eng.activation(gwf[:, :], gwf[:, :], AF.Exp, bias=gmax[:, 0:1],
                           accum_out=zsum[:, 0:1]).then_inc(s_s.h, 1)
        p.emit("scalar", sgate)
        sgn = s_s.inc()

        def vgate2(eng, i=i, sgn=sgn):
            W(eng, (s_s.h, sgn))
            eng.reciprocal(zsum[:, 1:2], zsum[:, 0:1])
            eng.tensor_scalar(gwf[:, :], gwf[:, :], zsum[:, 1:2], None, ALU.mult)
            eng.tensor_reduce(v1a[:, i:i + 1], gwf[:, :], AX.X, ALU.max)
            eng.tensor_scalar(oh1[:, i * E:(i + 1) * E], gwf[:, :], v1a[:, i:i + 1],
                              None, ALU.is_ge)
            eng.scalar_tensor_tensor(scr8[:, :], oh1[:, i * E:(i + 1) * E], -1e9,
                                     gwf[:, :], ALU.mult, ALU.add)
            eng.tensor_reduce(gmax[:, 1:2], scr8[:, :], AX.X, ALU.max)
            eng.tensor_scalar(ohv2[:, i * E:(i + 1) * E], scr8[:, :], gmax[:, 1:2],
                              None, ALU.is_ge)
            eng.tensor_scalar(ohv2[:, i * E:(i + 1) * E], ohv2[:, i * E:(i + 1) * E],
                              gmax[:, 1:2], None, ALU.mult).then_inc(s_v.h, 1)
        p.emit("vector", vgate2)
        pg_evac = s_v.inc()
    p2_v = s_v.n

    def vz(eng, need=p2_v):
        W(eng, (s_v.h, need))
        eng.memset(eo1[:, :], 0.0).then_inc(s_v.h, 1)
    p.emit("vector", vz)
    s_v.inc()

    f1_bank = {0: 0, 1: 0}
    f2_bank = {0: 0, 1: 0}

    def expert_pass(sel_buf, act_ready_v):
        for e in range(E):
            tneed = s_t.n
            for k in range(NK):
                p.dma(qw, "gpsimd", w1t[:, k * 2048:(k + 1) * 2048],
                      bass.AP(w1_d.tensor, e * D_MODEL * 2048 + k * 128 * 2048,
                              [[2048, 128], [1, 2048]]),
                      waits=[(s_t.h, tneed)] if k == 0 else [])
            w1all = tuple(qw.all_done())
            for hk in range(NH):
                p.dma(qw, "gpsimd", w2t[:, hk * D_MODEL:(hk + 1) * D_MODEL],
                      bass.AP(w2_d.tensor, e * 2048 * D_MODEL + hk * 128 * D_MODEL,
                              [[D_MODEL, 128], [1, D_MODEL]]),
                      waits=[(s_t.h, tneed)] if hk == 0 else [])
            w2all = tuple(qw.all_done())

            for g in range(NG):
                for hc in range(NH):
                    pb = pf1[hc % 2]

                    def tf1(eng, e=e, g=g, hc=hc, pb=pb, w1all=w1all,
                            evac=f1_bank[hc % 2], arv=act_ready_v):
                        W(eng, *w1all)
                        W(eng, (s_v.h, arv))
                        if evac:
                            eng.wait_ge(s_s.h, evac)
                        for k in range(NK):
                            ins = eng.matmul(
                                pb[:, :],
                                w1t[:, k * 2048 + hc * 128:k * 2048 + (hc + 1) * 128],
                                actT[:, k * LOC + g * 512:k * LOC + (g + 1) * 512],
                                start=(k == 0), stop=(k == NK - 1))
                        ins.then_inc(s_t.h, 1)
                    p.emit("tensor", tf1)
                    tf1n = s_t.inc()

                    def sf1(eng, e=e, hc=hc, pb=pb, tf1n=tf1n, c4=c4):
                        W(eng, (s_t.h, tf1n), c4)
                        eng.activation(h1T[:, hc * 512:(hc + 1) * 512], pb[:, :],
                                       AF.Relu, bias=b1t[:, e * NH + hc:e * NH + hc + 1]
                                       ).then_inc(s_s.h, 1)
                    p.emit("scalar", sf1)
                    f1_bank[hc % 2] = s_s.inc()
                f1_done = s_s.n

                for tg_ in range(4):
                    tile = g * 4 + tg_
                    for half in range(2):
                        pb = pf2[(tg_ * 2 + half) % 2]

                        def tf2(eng, e=e, tg_=tg_, half=half, pb=pb,
                                f1_done=f1_done, w2all=w2all, g2=g2, g4=g4,
                                evac=f2_bank[(tg_ * 2 + half) % 2]):
                            W(eng, (s_s.h, f1_done), g2, g4)
                            W(eng, *w2all)
                            if evac:
                                eng.wait_ge(s_v.h, evac)
                            for hk in range(NH):
                                eng.matmul(pb[:, :],
                                           h1T[:, hk * 512 + tg_ * 128:
                                               hk * 512 + (tg_ + 1) * 128],
                                           w2t[:, hk * D_MODEL + half * 512:
                                               hk * D_MODEL + half * 512 + 512],
                                           start=(hk == 0), stop=False)
                            eng.matmul(pb[:, :], onesb[:, :],
                                       b2t[0:1, e * D_MODEL + half * 512:
                                           e * D_MODEL + half * 512 + 512],
                                       start=False, stop=True).then_inc(s_t.h, 1)
                        p.emit("tensor", tf2)
                        tf2n = s_t.inc()

                        def vf2(eng, e=e, tile=tile, half=half, pb=pb, tf2n=tf2n,
                                sel_buf=sel_buf):
                            W(eng, (s_t.h, tf2n))
                            dst = eo1[:, tile * D_MODEL + half * 512:
                                      tile * D_MODEL + half * 512 + 512]
                            eng.scalar_tensor_tensor(
                                dst, pb[:, :], sel_buf[:, tile * E + e:tile * E + e + 1],
                                dst, ALU.mult, ALU.add).then_inc(s_v.h, 1)
                        p.emit("vector", vf2)
                        f2_bank[(tg_ * 2 + half) % 2] = s_v.inc()

    expert_pass(oh1, p1_v)
    pass1_v = s_v.n

    vb_of = []
    for i in range(NT):
        xb = xtile[i % 2]
        lw = [p1_xb[0], p1_xb[1]] if len(p1_xb) >= 2 else list(p1_xb)
        if i == 0:
            pass
        if i >= 2:
            lw = lw + [(s_v.h, vb_of[i - 2])]
        ldx = p.dma(qx, "sync", xb[:, :], xb_s[i * 128:(i + 1) * 128, :], waits=lw)

        def vb(eng, i=i, xb=xb, ldx=ldx, p1v=pass1_v, tfree=(s_t.n if i >= 1 else 0)):
            W(eng, ldx, (s_v.h, p1v))
            if tfree:
                eng.wait_ge(s_t.h, tfree)
            eng.scalar_tensor_tensor(xnb[:, :],
                                     eo1[:, i * D_MODEL:(i + 1) * D_MODEL], 0.5,
                                     xb[:, :], ALU.mult, ALU.add)
            eng.tensor_scalar(eo1[:, i * D_MODEL:(i + 1) * D_MODEL],
                              eo1[:, i * D_MODEL:(i + 1) * D_MODEL],
                              v1a[:, i:i + 1], None, ALU.mult).then_inc(s_v.h, 1)
        p.emit("vector", vb)
        vb_of.append(s_v.inc())

        def t2(eng, vbn=vb_of[-1]):
            W(eng, (s_v.h, vbn))
            for j in range(NK):
                ins = eng.transpose(ptr[:, j * 128:(j + 1) * 128],
                                    xnb[:, j * 128:(j + 1) * 128], ident[:, :])
            ins.then_inc(s_t.h, 1)
        p.emit("tensor", t2)
        t2n = s_t.inc()

        def v5(eng, i=i, t2n=t2n):
            W(eng, (s_t.h, t2n))
            eng.tensor_copy(
                actT[:, :].rearrange("p (s t) -> p s t", s=NK)[:, :, i * 128:(i + 1) * 128],
                ptr[:, :].rearrange("p (s t) -> p s t", s=NK),
            ).then_inc(s_v.h, 1)
        p.emit("vector", v5)
        s_v.inc()
    boost_v = s_v.n

    expert_pass(ohv2, boost_v)
    pass2_v = s_v.n

    v7_of = []
    ox_of = []
    for i in range(NT):
        xb = xtile[i % 2]
        ob = outb[i % 2]
        lw = []
        if i == 0:
            lw = [(s_v.h, pass2_v)]
        if i >= 2:
            lw = [(s_v.h, v7_of[i - 2])]
        ldx = p.dma(qx, "sync", xb[:, :], xb_s[i * 128:(i + 1) * 128, :], waits=lw)

        def v5b(eng, i=i, xb=xb, ldx=ldx, sfree=s_s.n, p2v=pass2_v):
            W(eng, ldx, (s_s.h, sfree), (s_v.h, p2v))
            eng.tensor_add(xb[:, :], xb[:, :], eo1[:, i * D_MODEL:(i + 1) * D_MODEL])
            eng.tensor_reduce(nm[:, 0:1], xb[:, :], AX.X, ALU.add, negate=True)
            eng.tensor_scalar(nm[:, 1:2], nm[:, 0:1], 1.0 / D_MODEL, None, ALU.mult)
            eng.tensor_scalar(xb[:, :], xb[:, :], nm[:, 1:2], None, ALU.add
                              ).then_inc(s_v.h, 1)
        p.emit("vector", v5b)
        v5n = s_v.inc()

        def s5(eng, v5n=v5n, xb=xb):
            W(eng, (s_v.h, v5n))
            eng.activation(scr_sq[:, :], xb[:, :], AF.Square, accum_out=ssq[:, 0:1])
            eng.activation(ssq[:, 1:2], ssq[:, 0:1], AF.Ln, bias=cst[:, 0:1],
                           scale=1.0 / D_MODEL)
            eng.activation(rstd[:, 0:1], ssq[:, 1:2], AF.Exp, scale=-0.5).then_inc(s_s.h, 1)
        p.emit("scalar", s5)
        s5n = s_s.inc()

        def v6b(eng, i=i, xb=xb, ob=ob, s5n=s5n, c1=c1, c2=c2,
                ofree=(ox_of[i - 2] if i >= 2 else None)):
            W(eng, (s_s.h, s5n), c1, c2)
            if ofree:
                W(eng, ofree)
            eng.scalar_tensor_tensor(ob[:, :], xb[:, :], rstd[:, 0:1], gamb[:, :],
                                     ALU.mult, ALU.mult)
            eng.tensor_add(ob[:, :], ob[:, :], betb[:, :]).then_inc(s_v.h, 1)
        p.emit("vector", v6b)
        v6n = s_v.inc()

        ldx4 = p.dma(qx, "sync", xb[:, :], x1_d[i * 128:(i + 1) * 128, :],
                     waits=[(s_v.h, v6n)])

        def v7(eng, i=i, xb=xb, ob=ob, ldx4=ldx4):
            W(eng, ldx4)
            eng.tensor_add(ob[:, :], ob[:, :], xb[:, :]).then_inc(s_v.h, 1)
        p.emit("vector", v7)
        v7_of.append(s_v.inc())

        ox_of.append(p.dma(qo, "sync", x2_o[i * 128:(i + 1) * 128, :], ob[:, :],
                           waits=[(s_v.h, v7_of[-1])]))

    def fin(eng, outs=tuple(qo.all_done())):
        W(eng, *outs)
    p.emit("sync", fin)
    return p.finish()


def run_LE(inputs, x1_shards):
    b1 = np.asarray(inputs["L0_ffn_b1"], np.float32)
    b1r = np.ascontiguousarray(b1.reshape(8, 16, 128).transpose(2, 0, 1))
    common = dict(
        gate_W=np.asarray(inputs["L0_ffn_gate_W"], np.float32),
        gate_b=np.asarray(inputs["L0_ffn_gate_b"], np.float32),
        W1=np.asarray(inputs["L0_ffn_W1"], np.float32),
        b1r=b1r,
        W2=np.asarray(inputs["L0_ffn_W2"], np.float32),
        b2=np.asarray(inputs["L0_ffn_b2"], np.float32),
        ln_gamma=np.asarray(inputs["L0_ffn_ln_gamma"], np.float32),
        ln_beta=np.asarray(inputs["L0_ffn_ln_beta"], np.float32),
        ident=np.eye(128, dtype=np.float32),
        ones1=np.ones((1, 128), dtype=np.float32),
    )
    in_maps = []
    for c in range(NCORES):
        m = dict(common)
        m["x1"] = np.ascontiguousarray(x1_shards[c])
        in_maps.append(m)
    nc = build_LE()
    res = run_bass_kernel_spmd(nc, in_maps, core_ids=list(range(NCORES)))
    return res.results



def build_LH():
    p = Prog("LH")
    nc = p.nc
    NT = LOC // 128
    x_d = dram_in(nc, "x", [LOC, D_MODEL])
    h_o = dram_out(nc, "h", [LOC, D_MODEL], F32)
    a = nc.alloc_sbuf_tensor
    xt = [a(f"xt{i}", [128, D_MODEL], F32) for i in range(2)]
    scr_sq = a("scr_sq", [128, D_MODEL], F32)
    nm = a("nm", [128, 2], F32)
    ssq = a("ssq", [128, 2], F32)
    rstd = a("rstd", [128, 1], F32)
    cst = a("cst", [128, 1], F32)
    qx = DmaQ(p, "qx", 2)
    qo = DmaQ(p, "qo", 2)
    s_v = p.sem("v")
    s_s = p.sem("s")

    def gcst(eng):
        eng.memset(cst[:, 0:1], EPS).then_inc(s_v.h, 1)
    p.emit("vector", gcst)
    cd = s_v.inc()

    v2_of = []
    out_of = []
    for i in range(NT):
        xb = xt[i % 2]
        lw = []
        if i >= 2:
            lw = [(s_v.h, v2_of[i - 2]), out_of[i - 2]]
        ldx = p.dma(qx, "sync", xb[:, :], x_d[i * 128:(i + 1) * 128, :], waits=lw)

        def v1(eng, xb=xb, ldx=ldx, cd=cd, sfree=s_s.n):
            W(eng, ldx, (s_s.h, sfree))
            eng.wait_ge(s_v.h, cd)
            eng.tensor_reduce(nm[:, 0:1], xb[:, :], AX.X, ALU.add, negate=True)
            eng.tensor_scalar(nm[:, 1:2], nm[:, 0:1], 1.0 / D_MODEL, None, ALU.mult)
            eng.tensor_scalar(xb[:, :], xb[:, :], nm[:, 1:2], None, ALU.add
                              ).then_inc(s_v.h, 1)
        p.emit("vector", v1)
        v1n = s_v.inc()

        def s1(eng, xb=xb, v1n=v1n):
            W(eng, (s_v.h, v1n))
            eng.activation(scr_sq[:, :], xb[:, :], AF.Square, accum_out=ssq[:, 0:1])
            eng.activation(ssq[:, 1:2], ssq[:, 0:1], AF.Ln, bias=cst[:, 0:1],
                           scale=1.0 / D_MODEL)
            eng.activation(rstd[:, 0:1], ssq[:, 1:2], AF.Exp, scale=-0.5).then_inc(s_s.h, 1)
        p.emit("scalar", s1)
        s1n = s_s.inc()

        def v2(eng, xb=xb, s1n=s1n):
            W(eng, (s_s.h, s1n))
            eng.tensor_scalar(xb[:, :], xb[:, :], rstd[:, 0:1], None, ALU.mult
                              ).then_inc(s_v.h, 1)
        p.emit("vector", v2)
        v2_of.append(s_v.inc())
        out_of.append(p.dma(qo, "sync", h_o[i * 128:(i + 1) * 128, :], xb[:, :],
                            waits=[(s_v.h, v2_of[-1])]))

    def fin(eng, outs=tuple(qo.all_done())):
        W(eng, *outs)
    p.emit("sync", fin)
    return p.finish()



BANDW = 768


def _band_segs4(ti):
    segs = []
    for s in range(4):
        col0 = s * 2048 + ti * 128 - 256
        lo = max(col0, 0)
        hi = min(col0 + BANDW, L)
        segs.append((s * BANDW + (lo - col0), lo, hi - lo))
    return segs


def build_LF():
    p = Prog("LF")
    nc = p.nc
    NT = LOC // 128
    NG = 4
    HGRP = 512

    hf_d = dram_in(nc, "hfull", [L, D_MODEL])
    hl_d = dram_in(nc, "hlocal", [LOC, D_MODEL])
    wq_d = dram_in(nc, "WqT", [D_MODEL, D_MODEL])
    wk_d = dram_in(nc, "WkT", [D_MODEL, D_MODEL])
    bq_d = dram_in(nc, "bqp", [128, 8])
    bk_d = dram_in(nc, "bkp", [128, 8])
    lnw_d = dram_in(nc, "lnw6", [6, D_MODEL])
    w1_d = dram_in(nc, "W1T", [D_MODEL, 4096])
    b1_d = dram_in(nc, "b1p", [128, 32])
    w2_d = dram_in(nc, "W2n", [4096, D_MODEL])
    b2_d = dram_in(nc, "b2row", [1, D_MODEL])
    band_d = dram_in(nc, "band", [NT, 128, 4 * BANDW])
    ident_d = dram_in(nc, "ident", [128, 128])
    ones_d = dram_in(nc, "ones1", [1, 128])

    hn_o = dram_out(nc, "hnext", [LOC, D_MODEL], F32)

    kT_d = nc.dram_tensor("kT_scr", [D_MODEL, L], BF16, kind="Internal").ap()
    qT_d = nc.dram_tensor("qT_scr", [D_MODEL, LOC], BF16, kind="Internal").ap()
    h2f_d = nc.dram_tensor("h2f_scr", [LOC, D_MODEL], F32, kind="Internal").ap()
    h2b_d = nc.dram_tensor("h2b_scr", [LOC, D_MODEL], BF16, kind="Internal").ap()

    a = nc.alloc_sbuf_tensor
    lnw = a("lnw", [128, 6 * D_MODEL], BF16)
    bqp = a("bqp_sb", [128, 8], F32)
    bkp = a("bkp_sb", [128, 8], F32)
    b1p = a("b1p_sb", [128, 32], F32)
    b2r = a("b2r_sb", [1, D_MODEL], BF16)
    onesb = a("onesb", [1, 128], BF16)
    ident = a("ident_sb", [128, 128], BF16)
    cst = a("cst", [128, 2], F32)
    nm = a("nm", [128, 2], F32)
    ssq = a("ssq", [128, 2], F32)
    rstd = a("rstd", [128, 1], F32)
    zparts = [a(f"zparts{i}", [128, 16], F32) for i in range(2)]
    invz = [a(f"invz{i}", [128, 4], F32) for i in range(2)]
    relb = a("relb", [128, 4 * BANDW], BF16)
    tbseg = a("tbseg", [128, BANDW], BF16)
    hloc = [a(f"hloc{i}", [128, D_MODEL], F32) for i in range(2)]
    xa = a("xa", [128, D_MODEL], F32)
    xb2 = a("xb2", [128, D_MODEL], F32)
    h2buf = a("h2buf", [128, D_MODEL], F32)
    outb = [a(f"outb{i}", [128, D_MODEL], F32) for i in range(2)]
    z2a = a("z2a", [128, 2], F32)

    marker = a("arena_marker", [128, 16], F32)
    AB = nc.lookup_mloc("arena_marker").addr + 64
    AB = (AB + 31) & ~31
    K16 = 16384
    ASZ = 148 * 1024
    assert AB + ASZ <= 16512 + 212832, f"sbuf arena over: {AB + ASZ - (16512 + 212832)}"
    aat = nc.alloc_sbuf_tensor_at
    hTh = aat("hTh", [128, 8 * 4096], BF16, offset=AB)
    wq_sb = aat("wq_sb", [128, 8 * D_MODEL], BF16, offset=AB + 4 * K16)
    wk_sb = aat("wk_sb", [128, 8 * D_MODEL], BF16, offset=AB + 5 * K16)
    kstage = [aat(f"kstage{i}", [128, 2048], BF16, offset=AB + 6 * K16 + i * 4096)
              for i in range(2)]
    ht = [aat(f"ht{i}", [128, D_MODEL], F32, offset=AB + 6 * K16 + 8192 + i * 4096)
          for i in range(2)]
    hb = [aat(f"hb{i}", [128, D_MODEL], BF16, offset=AB + 7 * K16 + i * 2048)
          for i in range(2)]
    hTl = aat("hTl", [128, 8 * LOC], BF16, offset=AB + 116 * 1024)
    wn = [aat(f"wn{i}", [128, L], BF16, offset=AB + i * K16) for i in range(4)]
    pbufE = [aat(f"pbufE{i}", [128, L], BF16, offset=AB + (4 + 2 * i) * K16)
             for i in range(2)]
    pbufO = [aat(f"pbufO{i}", [128, L], BF16, offset=AB + (5 + 2 * i) * K16)
             for i in range(2)]
    kpairH = [aat(f"kpairH{i}", [128, 4096], BF16, offset=AB + 8 * K16 + i * 8192)
              for i in range(2)]
    q_t4 = [aat(f"q_t4_{i}", [128, 8 * 128], BF16, offset=AB + 9 * K16 + i * 2048)
            for i in range(2)]
    hstr = [aat(f"hstr{i}", [128, 4 * D_MODEL], BF16, offset=AB + 4 * K16 + i * 8192)
            for i in range(2)]
    scr_sq = aat("scr_sq", [128, D_MODEL], F32, offset=AB + 6 * K16)
    w1q = [aat(f"w1q{i}", [128, 8 * 1024], BF16, offset=AB + i * K16) for i in range(2)]
    w2q = [aat(f"w2q{i}", [128, 8 * 1024], BF16, offset=AB + (2 + i) * K16)
           for i in range(2)]
    h1T = [aat(f"h1T{i}", [128, 8 * 512], BF16, offset=AB + 7 * K16 + i * 8192)
           for i in range(2)]
    h2Tg = aat("h2Tg", [128, 8 * 512], BF16, offset=AB + 8 * K16)
    ffacc = aat("ffacc", [128, 4 * D_MODEL], F32, offset=AB + 4 * K16)

    qc = DmaQ(p, "qc", 2)
    qh = DmaQ(p, "qh", 3)
    qw = DmaQ(p, "qw", 3)
    qs = DmaQ(p, "qs", 3)
    qt = DmaQ(p, "qt", 2)
    qo = DmaQ(p, "qo", 2)
    s_v = p.sem("v")
    s_s = p.sem("s")
    s_t = p.sem("t")

    lnw_cp = []
    for j in range(6):
        lw = [(s_v.h, lnw_cp[-1])] if lnw_cp else []
        ld_j = p.dma(qc, "sync", xa[:, :],
                     bcast_ap(lnw_d, D_MODEL, offset=j * D_MODEL), waits=lw)

        def vlnw(eng, j=j, ld_j=ld_j):
            W(eng, ld_j)
            eng.tensor_copy(lnw[:, j * D_MODEL:(j + 1) * D_MODEL],
                            xa[:, :]).then_inc(s_v.h, 1)
        p.emit("vector", vlnw)
        lnw_cp.append(s_v.inc())
    p.dma(qc, "sync", bqp[:, :], bq_d)
    p.dma(qc, "sync", bkp[:, :], bk_d)
    p.dma(qc, "sync", b1p[:, :], b1_d)
    consts_done = tuple(qc.all_done())

    g_id = p.dma(qw, "gpsimd", ident[:, :], ident_d)
    g_b2 = p.dma(qw, "gpsimd", b2r[:, :], b2_d)
    g_on = p.dma(qw, "gpsimd", onesb[:, :], ones_d)
    for k in range(8):
        p.dma(qw, "gpsimd", wq_sb[:, k * D_MODEL:(k + 1) * D_MODEL],
              wq_d[k * 128:(k + 1) * 128, :])
    for k in range(8):
        p.dma(qw, "gpsimd", wk_sb[:, k * D_MODEL:(k + 1) * D_MODEL],
              wk_d[k * 128:(k + 1) * 128, :])
    wqk_done = tuple(qw.all_done())

    def gcst(eng):
        eng.memset(cst[:, 0:1], EPS)
        eng.memset(cst[:, 1:2], -80.0).then_inc(s_v.h, 1)
    p.emit("vector", gcst)
    cst_done_v = s_v.inc()

    with nc.psum_tensor("ptr_qk", [128, 1024], BF16) as ptr, \
         nc.psum_tensor("pk0", [128, 512], F32) as pk0, \
         nc.psum_tensor("pk1", [128, 512], F32) as pk1:
        pk = [pk0, pk1]
        kst_dma = {0: None, 1: None}
        sc_of = []
        v1_of = []
        tix = [0]

        def tp_tile(src_d, row0, dst_view):
            ji = tix[0]
            tix[0] += 1
            lw = []
            if ji >= 2:
                lw = [(s_s.h, sc_of[ji - 2])]
            ldt = p.dma(qh, "sync", ht[ji % 2][:, :], src_d[row0:row0 + 128, :],
                        waits=lw)

            def sc(eng, buf=ht[ji % 2], ji=ji, ldt=ldt,
                   tprev=(s_t.n if ji >= 2 else 0)):
                W(eng, ldt)
                if tprev:
                    eng.wait_ge(s_t.h, tprev)
                eng.copy(hb[ji % 2][:, :], buf[:, :]).then_inc(s_s.h, 1)
            p.emit("scalar", sc)
            sc_of.append(s_s.inc())

            def t1(eng, ji=ji, scn=sc_of[-1], g_id=g_id,
                   vprev=(v1_of[-1] if v1_of else 0)):
                W(eng, (s_s.h, scn), g_id)
                if vprev:
                    eng.wait_ge(s_v.h, vprev)
                for sl in range(8):
                    ins = eng.transpose(ptr[:, sl * 128:(sl + 1) * 128],
                                        hb[ji % 2][:, sl * 128:(sl + 1) * 128],
                                        ident[:, :])
                ins.then_inc(s_t.h, 1)
            p.emit("tensor", t1)
            t1n = s_t.inc()

            def v1(eng, t1n=t1n, dst_view=dst_view):
                W(eng, (s_t.h, t1n))
                eng.tensor_copy(dst_view,
                                ptr[:, :].rearrange("p (s t) -> p s t", s=8)
                                ).then_inc(s_v.h, 1)
            p.emit("vector", v1)
            v1_of.append(s_v.inc())

        def proj(wsb, bias, dst_d, ntc, colbase, rdy_v, rhs_buf, rhs_w):
            for fc in range(8):
                for tc in range(ntc):
                    pb = pk[tc % 2]

                    def tmm(eng, fc=fc, tc=tc, pb=pb, wsb=wsb, rdy_v=rdy_v,
                            rhs_buf=rhs_buf, rhs_w=rhs_w,
                            wd=wqk_done, evac=(s_s.n - 1 if s_s.n > 1 else 0)):
                        W(eng, (s_v.h, rdy_v), *wd)
                        if evac > 0:
                            eng.wait_ge(s_s.h, evac)
                        for k in range(8):
                            ins = eng.matmul(
                                pb[:, :],
                                wsb[:, k * D_MODEL + fc * 128:k * D_MODEL + (fc + 1) * 128],
                                rhs_buf[:, k * rhs_w + tc * 512:k * rhs_w + (tc + 1) * 512],
                                start=(k == 0), stop=(k == 7))
                        ins.then_inc(s_t.h, 1)
                    p.emit("tensor", tmm)
                    mmn = s_t.inc()

                    def sev(eng, tc=tc, pb=pb, mmn=mmn, bias=bias, fc=fc,
                            cdone=consts_done,
                            stfree=(kst_dma[(tc // 4) % 2] if tc % 4 == 0 else None)):
                        W(eng, (s_t.h, mmn), *cdone)
                        if stfree:
                            W(eng, stfree)
                        eng.activation(kstage[(tc // 4) % 2][:, (tc % 4) * 512:
                                                             (tc % 4) * 512 + 512],
                                       pb[:, :], AF.Identity,
                                       bias=bias[:, fc:fc + 1]).then_inc(s_s.h, 1)
                    p.emit("scalar", sev)
                    sevn = s_s.inc()

                    if tc % 4 == 3:
                        col0 = colbase + (tc // 4) * 2048
                        kst_dma[(tc // 4) % 2] = p.dma(
                            qs, "gpsimd",
                            dst_d[fc * 128:(fc + 1) * 128, col0:col0 + 2048],
                            kstage[(tc // 4) % 2][:, :], waits=[(s_s.h, sevn)])

        hTl_r = hTl[:, :].rearrange("p (s t) -> p s t", s=8)
        for jt in range(16):
            tp_tile(hl_d, jt * 128, hTl_r[:, :, jt * 128:(jt + 1) * 128])
        loc_v = s_v.n
        proj(wq_sb, bqp, qT_d, 4, 0, loc_v, hTl, LOC)

        hTh_r = hTh[:, :].rearrange("p (s t) -> p s t", s=8)
        for hh in range(2):
            for j in range(32):
                tp_tile(hf_d, hh * 4096 + j * 128,
                        hTh_r[:, :, j * 128:(j + 1) * 128])
            proj(wk_sb, bkp, kT_d, 8, hh * 4096, s_v.n, hTh, 4096)
    qk_stores = tuple(qs.all_done())
    qk_v = s_v.n

    d_of = []
    h2_stores = []
    _lf_state = {"btr": 0}
    for g in range(NG):
        with nc.psum_tensor(f"psA{g}", [128, 1024], F32) as psA, \
             nc.psum_tensor(f"psB{g}", [128, 1024], F32) as psB, \
             nc.psum_tensor(f"psC{g}", [128, 1024], F32) as psC, \
             nc.psum_tensor(f"ptrB{g}", [128, 1024], BF16) as ptrB:
            psq = [psA, psB, psC, psC]

            def vz(eng, qkv=qk_v, dprev=(d_of[g - 1] if g else 0), qst=qk_stores):
                eng.wait_ge(s_v.h, max(qkv, dprev))
                W(eng, *qst)
                for t4 in range(4):
                    ins = eng.memset(wn[t4][:, :], 0.0)
                ins.then_inc(s_v.h, 1)
            p.emit("vector", vz)
            wn0 = s_v.inc()

            bank_ev = {0: 0, 1: 0, 2: 0}
            stt_of = {0: 0, 1: 0}
            khalf_rd = _lf_state.setdefault("khalf_rd", {0: 0, 1: 0})
            qrd = _lf_state.setdefault("qrd", {0: 0, 1: 0})
            for t4 in range(4):
                ti = g * 4 + t4
                qw_ = [(s_t.h, qrd[t4 % 2])] if qrd[t4 % 2] else [(s_v.h, wn0), *qk_stores]
                qld = p.dma(qh, "sync", q_t4[t4 % 2][:, :],
                            bass.AP(qT_d.tensor, ti * 128,
                                    [[LOC, 128], [128 * LOC, 8], [1, 128]]),
                            waits=qw_)

                for hd in range(16):
                    hp = hd // 2
                    odd = hd % 2
                    par = hd % 2
                    pbuf = pbufE[par]
                    zp, iv = zparts[par], invz[par]
                    if odd == 0:
                        kldh = []
                        for half in range(2):
                            klw = [(s_t.h, khalf_rd[half])] if khalf_rd[half] else \
                                [(s_v.h, wn0), *qk_stores]
                            kldh.append(p.dma(
                                qh, "sync", kpairH[half][:, :],
                                kT_d[hp * 128:(hp + 1) * 128,
                                     half * 4096:(half + 1) * 4096],
                                waits=klw))
                        _lf_state["kldh"] = kldh
                    else:
                        kldh = _lf_state["kldh"]

                    if hd == 0:
                        pfree0 = ("t", _lf_state["btr"])
                    elif hd == 1:
                        pfree0 = ("v", stt_of[1])
                    else:
                        pfree0 = ("v", stt_of[par])

                    po = 64 * odd
                    tpos = (64, 0) if odd else None
                    ev_of = []
                    for mcq in range(8):
                        pbs = psq[mcq % 3]

                        def tsc(eng, t4=t4, hp=hp, po=po, tpos=tpos, mcq=mcq, pbs=pbs,
                                kld=kldh[mcq // 4], qld=qld, wn0=wn0,
                                ev=bank_ev[mcq % 3],
                                pfree=(pfree0 if mcq == 0 else None)):
                            W(eng, kld, qld, (s_v.h, wn0))
                            if ev:
                                eng.wait_ge(s_s.h, ev)
                            if pfree and pfree[1]:
                                if pfree[0] == "v":
                                    eng.wait_ge(s_v.h, pfree[1])
                                else:
                                    eng.wait_ge(s_t.h, pfree[1])
                            for j in range(2):
                                ins = eng.matmul(
                                    pbs[:, j * 512:(j + 1) * 512],
                                    q_t4[t4 % 2][po:po + 64, hp * 128:hp * 128 + 128],
                                    kpairH[mcq // 4][po:po + 64,
                                        (mcq % 4) * 1024 + j * 512:
                                        (mcq % 4) * 1024 + (j + 1) * 512],
                                    start=True, stop=True, tile_position=tpos)
                            ins.then_inc(s_t.h, 1)
                        p.emit("tensor", tsc)
                        mmn = s_t.inc()
                        if mcq % 4 == 3 and odd == 1:
                            khalf_rd[mcq // 4] = mmn

                        def sev(eng, mcq=mcq, mmn=mmn, pbs=pbs, pbuf=pbuf,
                                zp=zp, cd=cst_done_v):
                            W(eng, (s_t.h, mmn))
                            eng.wait_ge(s_v.h, cd)
                            eng.activation(pbuf[:, mcq * 1024:(mcq + 1) * 1024],
                                           pbs[:, :], AF.Exp, scale=0.125,
                                           bias=cst[:, 1:2],
                                           accum_out=zp[:, mcq:mcq + 1]
                                           ).then_inc(s_s.h, 1)
                        p.emit("scalar", sev)
                        evn = s_s.inc()
                        ev_of.append(evn)
                        bank_ev[mcq % 3] = evn

                    def vacc(eng, t4=t4, evn=ev_of[-1], pbuf=pbuf, zp=zp, iv=iv):
                        W(eng, (s_s.h, evn))
                        eng.tensor_reduce(iv[:, 0:1], zp[:, 0:8], AX.X, ALU.add)
                        eng.reciprocal(iv[:, 2:3], iv[:, 0:1])
                        eng.tensor_scalar(iv[:, 2:3], iv[:, 2:3], 1.0 / 16, None,
                                          ALU.mult)
                        eng.tensor_scalar(pbuf[:, :], pbuf[:, :], iv[:, 2:3],
                                          None, ALU.mult)
                        eng.tensor_add(wn[t4][:, :], wn[t4][:, :], pbuf[:, :]
                                       ).then_inc(s_v.h, 1)
                    p.emit("vector", vacc)
                    stt_of[par] = s_v.inc()
                qrd[t4 % 2] = s_t.n

                pbE, pbO = pbufE[0], pbufO[0]
                rld = p.dma(qs, "gpsimd", relb[:, :], band_d[ti, :, :],
                            waits=[(s_v.h, s_v.n)])

                def sB(eng, t4=t4, pbE=pbE, vprev=s_v.n):
                    eng.wait_ge(s_v.h, vprev)
                    eng.activation(pbE[:, :], wn[t4][:, :], AF.Exp,
                                   accum_out=z2a[:, 0:1]).then_inc(s_s.h, 1)
                p.emit("scalar", sB)
                sBn = s_s.inc()

                segs = _band_segs4(ti)

                def vB(eng, sBn=sBn, rld=rld, segs=segs, pbE=pbE, pbO=pbO):
                    W(eng, (s_s.h, sBn), rld)
                    for toff, col0, wid in segs:
                        eng.tensor_mul(tbseg[:, 0:wid],
                                       pbE[:, col0:col0 + wid],
                                       relb[:, toff:toff + wid])
                        eng.tensor_add(pbE[:, col0:col0 + wid],
                                       pbE[:, col0:col0 + wid],
                                       tbseg[:, 0:wid])
                        eng.tensor_reduce(z2a[:, 1:2], tbseg[:, 0:wid], AX.X, ALU.add)
                        eng.tensor_add(z2a[:, 0:1], z2a[:, 0:1], z2a[:, 1:2])
                    eng.reciprocal(invz[0][:, 0:1], z2a[:, 0:1])
                    eng.tensor_scalar(pbO[:, :], pbE[:, :], invz[0][:, 0:1],
                                      None, ALU.mult).then_inc(s_v.h, 1)
                p.emit("vector", vB)
                vBn = s_v.inc()
                stt_of[0] = vBn

                for tb8 in range(8):
                    def tB(eng, t4=t4, tb8=tb8, vBn=vBn, pbO=pbO, vprev=s_v.n):
                        eng.wait_ge(s_v.h, max(vBn, vprev))
                        for j in range(8):
                            c = tb8 * 8 + j
                            ins = eng.transpose(ptrB[:, j * 128:(j + 1) * 128],
                                                pbO[:, c * 128:(c + 1) * 128],
                                                ident[:, :])
                        ins.then_inc(s_t.h, 1)
                    p.emit("tensor", tB)
                    tBn = s_t.inc()

                    def vBe(eng, t4=t4, tb8=tb8, tBn=tBn):
                        W(eng, (s_t.h, tBn))
                        eng.tensor_copy(wn[t4][:, tb8 * 1024:(tb8 + 1) * 1024],
                                        ptrB[:, :]).then_inc(s_v.h, 1)
                    p.emit("vector", vBe)
                    s_v.inc()
                _lf_state["btr"] = s_t.n
        b_done = s_v.n

        with nc.psum_tensor(f"psa{g}", [128, 8 * 512], F32) as psa:
            tC_of = []
            for ms in range(16):
                hw_ = [(s_v.h, b_done)] if ms < 2 else [(s_t.h, tC_of[ms - 2])]
                hld = p.dma(qs, "gpsimd", hstr[ms % 2][:, :],
                            bass.AP(hf_d.tensor, ms * 512 * D_MODEL,
                                    [[D_MODEL, 128], [128 * D_MODEL, 4], [1, D_MODEL]]),
                            waits=hw_)

                def tC(eng, ms=ms, hld=hld, bdone=b_done):
                    W(eng, hld, (s_v.h, bdone))
                    for mc4 in range(4):
                        m128 = ms * 4 + mc4
                        for t4 in range(4):
                            for half in range(2):
                                ins = eng.matmul(
                                    psa[:, (t4 * 2 + half) * 512:(t4 * 2 + half + 1) * 512],
                                    wn[t4][:, m128 * 128:(m128 + 1) * 128],
                                    hstr[ms % 2][:, mc4 * D_MODEL + half * 512:
                                                 mc4 * D_MODEL + half * 512 + 512],
                                    start=(m128 == 0), stop=(m128 == 63))
                    ins.then_inc(s_t.h, 1)
                p.emit("tensor", tC)
                tC_of.append(s_t.inc())
            c_done = s_t.n

            for t4 in range(4):
                ti = g * 4 + t4
                hldD = p.dma(qh, "sync", hloc[t4 % 2][:, :],
                             hl_d[ti * 128:(ti + 1) * 128, :],
                             waits=[(s_v.h, s_v.n)])

                def vD1(eng, t4=t4, cdn=c_done, hldD=hldD, cd=consts_done, sfree=s_s.n):
                    W(eng, (s_t.h, cdn), hldD, *cd)
                    eng.wait_ge(s_s.h, sfree)
                    eng.tensor_add(xa[:, 0:512], hloc[t4 % 2][:, 0:512],
                                   psa[:, t4 * 1024:t4 * 1024 + 512])
                    eng.tensor_add(xa[:, 512:1024], hloc[t4 % 2][:, 512:1024],
                                   psa[:, t4 * 1024 + 512:t4 * 1024 + 1024])
                    eng.tensor_reduce(nm[:, 0:1], xa[:, :], AX.X, ALU.add, negate=True)
                    eng.tensor_scalar(nm[:, 1:2], nm[:, 0:1], 1.0 / D_MODEL, None, ALU.mult)
                    eng.tensor_scalar(xa[:, :], xa[:, :], nm[:, 1:2], None, ALU.add
                                      ).then_inc(s_v.h, 1)
                p.emit("vector", vD1)
                vD1n = s_v.inc()

                def sD1(eng, vD1n=vD1n):
                    W(eng, (s_v.h, vD1n))
                    eng.activation(scr_sq[:, :], xa[:, :], AF.Square, accum_out=ssq[:, 0:1])
                    eng.activation(ssq[:, 1:2], ssq[:, 0:1], AF.Ln, bias=cst[:, 0:1],
                                   scale=1.0 / D_MODEL)
                    eng.activation(rstd[:, 0:1], ssq[:, 1:2], AF.Exp, scale=-0.5
                                   ).then_inc(s_s.h, 1)
                p.emit("scalar", sD1)
                sD1n = s_s.inc()

                def vD2(eng, sD1n=sD1n):
                    W(eng, (s_s.h, sD1n))
                    eng.scalar_tensor_tensor(xb2[:, :], xa[:, :], rstd[:, 0:1],
                                             lnw[:, 0:D_MODEL], ALU.mult, ALU.mult)
                    eng.tensor_add(xb2[:, :], xb2[:, :], lnw[:, D_MODEL:2 * D_MODEL])
                    eng.tensor_reduce(nm[:, 0:1], xb2[:, :], AX.X, ALU.add, negate=True)
                    eng.tensor_scalar(nm[:, 1:2], nm[:, 0:1], 1.0 / D_MODEL, None, ALU.mult)
                    eng.tensor_scalar(xb2[:, :], xb2[:, :], nm[:, 1:2], None, ALU.add
                                      ).then_inc(s_v.h, 1)
                p.emit("vector", vD2)
                vD2n = s_v.inc()

                def sD2(eng, vD2n=vD2n):
                    W(eng, (s_v.h, vD2n))
                    eng.activation(scr_sq[:, :], xb2[:, :], AF.Square, accum_out=ssq[:, 0:1])
                    eng.activation(ssq[:, 1:2], ssq[:, 0:1], AF.Ln, bias=cst[:, 0:1],
                                   scale=1.0 / D_MODEL)
                    eng.activation(rstd[:, 0:1], ssq[:, 1:2], AF.Exp, scale=-0.5
                                   ).then_inc(s_s.h, 1)
                p.emit("scalar", sD2)
                sD2n = s_s.inc()

                def vD3(eng, sD2n=sD2n,
                        h2free=(h2_stores[-1] if h2_stores else None)):
                    W(eng, (s_s.h, sD2n))
                    if h2free:
                        W(eng, *h2free)
                    eng.scalar_tensor_tensor(h2buf[:, :], xb2[:, :], rstd[:, 0:1],
                                             lnw[:, 2 * D_MODEL:3 * D_MODEL],
                                             ALU.mult, ALU.mult)
                    eng.tensor_add(h2buf[:, :], h2buf[:, :],
                                   lnw[:, 3 * D_MODEL:4 * D_MODEL]).then_inc(s_v.h, 1)
                p.emit("vector", vD3)
                vD3n = s_v.inc()

                st1 = p.dma(qt, "scalar", h2f_d[ti * 128:(ti + 1) * 128, :], h2buf[:, :],
                            waits=[(s_v.h, vD3n)])
                st2 = p.dma(qs, "gpsimd", h2b_d[ti * 128:(ti + 1) * 128, :], h2buf[:, :],
                            waits=[(s_v.h, vD3n)])
                h2_stores.append((st1, st2))
        d_of.append(s_v.n)
    h2_all = tuple(qt.all_done()) + tuple(qs.all_done())

    with nc.psum_tensor("pff1a", [128, 512], F32) as pf1a, \
         nc.psum_tensor("pff1b", [128, 512], F32) as pf1b, \
         nc.psum_tensor("pff2a", [128, 512], F32) as pf2a, \
         nc.psum_tensor("pff2b", [128, 512], F32) as pf2b:
        pf1 = [pf1a, pf1b]
        pf2 = [pf2a, pf2b]
        f1_bank = {0: 0, 1: 0}
        f2_bank = {0: 0, 1: 0}
        out_of = []
        w_of = {0: None, 1: None}
        h1_used = {0: 0, 1: 0}
        for g in range(NG):
            htg = []
            for sl in range(8):
                tw = [*h2_all, (s_t.h, s_t.n)] if sl == 0 else []
                htg.append(p.dma(qt, "scalar", h2Tg[:, sl * 512:(sl + 1) * 512],
                                 h2b_d[g * 512:(g + 1) * 512, sl * 128:(sl + 1) * 128],
                                 waits=tw, transpose=True))
            htg_done = tuple(qt.all_done())

            for qtr in range(4):
                par = qtr % 2
                wlw = [w_of[par]] if w_of[par] else [(s_t.h, s_t.n)]
                for k in range(8):
                    p.dma(qw, "gpsimd", w1q[par][:, k * 1024:(k + 1) * 1024],
                          bass.AP(w1_d.tensor, k * 128 * 4096 + qtr * 1024,
                                  [[4096, 128], [1, 1024]]),
                          waits=wlw if k == 0 else [])
                for nk in range(8):
                    p.dma(qw, "gpsimd", w2q[par][:, nk * 1024:(nk + 1) * 1024],
                          w2_d[qtr * 1024 + nk * 128:qtr * 1024 + (nk + 1) * 128, :],
                          waits=[])
                w_all = tuple(qw.all_done())

                for nk in range(8):
                    pb = pf1[nk % 2]

                    def tF1(eng, qtr=qtr, nk=nk, pb=pb, par=par, wa=w_all,
                            htgd=htg_done, evac=f1_bank[nk % 2],
                            h1free=(h1_used[par] if nk == 0 else 0)):
                        W(eng, *wa)
                        W(eng, *htgd)
                        if evac:
                            eng.wait_ge(s_s.h, evac)
                        if h1free:
                            eng.wait_ge(s_t.h, h1free)
                        for k in range(8):
                            ins = eng.matmul(pb[:, :],
                                             w1q[par][:, k * 1024 + nk * 128:
                                                 k * 1024 + (nk + 1) * 128],
                                             h2Tg[:, k * 512:(k + 1) * 512],
                                             start=(k == 0), stop=(k == 7))
                        ins.then_inc(s_t.h, 1)
                    p.emit("tensor", tF1)
                    tF1n = s_t.inc()

                    def sF1(eng, qtr=qtr, nk=nk, pb=pb, par=par, tF1n=tF1n,
                            cd=consts_done):
                        W(eng, (s_t.h, tF1n), *cd)
                        eng.activation(h1T[par][:, nk * 512:(nk + 1) * 512], pb[:, :],
                                       AF.Relu,
                                       bias=b1p[:, qtr * 8 + nk:qtr * 8 + nk + 1]
                                       ).then_inc(s_s.h, 1)
                    p.emit("scalar", sF1)
                    f1_bank[nk % 2] = s_s.inc()
                f1_done = s_s.n

                for t4 in range(4):
                    for eh in range(2):
                        pb = pf2[(t4 * 2 + eh) % 2]

                        def tF2(eng, qtr=qtr, t4=t4, eh=eh, pb=pb, par=par,
                                f1d=f1_done, gb2=g_b2, gon=g_on,
                                evac=f2_bank[(t4 * 2 + eh) % 2]):
                            W(eng, (s_s.h, f1d), gb2, gon)
                            if evac:
                                eng.wait_ge(s_v.h, evac)
                            for nk in range(8):
                                ins = eng.matmul(pb[:, :],
                                                 h1T[par][:, nk * 512 + t4 * 128:
                                                     nk * 512 + (t4 + 1) * 128],
                                                 w2q[par][:, nk * 1024 + eh * 512:
                                                     nk * 1024 + eh * 512 + 512],
                                                 start=(nk == 0),
                                                 stop=(qtr < 3 and nk == 7))
                            if qtr == 3:
                                ins = eng.matmul(pb[:, :], onesb[:, :],
                                                 b2r[0:1, eh * 512:eh * 512 + 512],
                                                 start=False, stop=True)
                            ins.then_inc(s_t.h, 1)
                        p.emit("tensor", tF2)
                        tF2n = s_t.inc()
                        h1_used[par] = tF2n

                        def vF2(eng, qtr=qtr, t4=t4, eh=eh, pb=pb, tF2n=tF2n):
                            W(eng, (s_t.h, tF2n))
                            dst = ffacc[:, t4 * D_MODEL + eh * 512:
                                        t4 * D_MODEL + eh * 512 + 512]
                            if qtr == 0:
                                eng.tensor_copy(dst, pb[:, :]).then_inc(s_v.h, 1)
                            else:
                                eng.tensor_add(dst, dst, pb[:, :]).then_inc(s_v.h, 1)
                        p.emit("vector", vF2)
                        f2_bank[(t4 * 2 + eh) % 2] = s_v.inc()
                w_of[par] = (s_t.h, s_t.n)
            ff_done = s_v.n

            for t4 in range(4):
                ti = g * 4 + t4
                h2ld = p.dma(qh, "sync", hloc[t4 % 2][:, :],
                             h2f_d[ti * 128:(ti + 1) * 128, :],
                             waits=[(s_v.h, s_v.n)])

                def vE1(eng, t4=t4, ffd=ff_done, h2ld=h2ld, sfree=s_s.n):
                    W(eng, (s_v.h, ffd), h2ld, (s_s.h, sfree))
                    eng.tensor_add(xa[:, :], hloc[t4 % 2][:, :],
                                   ffacc[:, t4 * D_MODEL:(t4 + 1) * D_MODEL])
                    eng.tensor_reduce(nm[:, 0:1], xa[:, :], AX.X, ALU.add, negate=True)
                    eng.tensor_scalar(nm[:, 1:2], nm[:, 0:1], 1.0 / D_MODEL, None, ALU.mult)
                    eng.tensor_scalar(xa[:, :], xa[:, :], nm[:, 1:2], None, ALU.add
                                      ).then_inc(s_v.h, 1)
                p.emit("vector", vE1)
                vE1n = s_v.inc()

                def sE1(eng, vE1n=vE1n):
                    W(eng, (s_v.h, vE1n))
                    eng.activation(scr_sq[:, :], xa[:, :], AF.Square, accum_out=ssq[:, 0:1])
                    eng.activation(ssq[:, 1:2], ssq[:, 0:1], AF.Ln, bias=cst[:, 0:1],
                                   scale=1.0 / D_MODEL)
                    eng.activation(rstd[:, 0:1], ssq[:, 1:2], AF.Exp, scale=-0.5
                                   ).then_inc(s_s.h, 1)
                p.emit("scalar", sE1)
                sE1n = s_s.inc()

                def vE2(eng, t4=t4, sE1n=sE1n,
                        ofree=(out_of[-2] if len(out_of) >= 2 else None)):
                    W(eng, (s_s.h, sE1n))
                    if ofree:
                        W(eng, ofree)
                    ob = outb[t4 % 2]
                    eng.scalar_tensor_tensor(ob[:, :], xa[:, :], rstd[:, 0:1],
                                             lnw[:, 4 * D_MODEL:5 * D_MODEL],
                                             ALU.mult, ALU.mult)
                    eng.tensor_add(ob[:, :], ob[:, :], lnw[:, 5 * D_MODEL:6 * D_MODEL]
                                   ).then_inc(s_v.h, 1)
                p.emit("vector", vE2)
                vE2n = s_v.inc()

                out_of.append(p.dma(qo, "sync", hn_o[ti * 128:(ti + 1) * 128, :],
                                    outb[t4 % 2][:, :], waits=[(s_v.h, vE2n)]))

    def fin(eng, outs=tuple(qo.all_done())):
        W(eng, *outs)
    p.emit("sync", fin)
    return p.finish()



FP4_THR = [0.25, 0.75, 1.25, 1.75, 2.5, 3.5, 5.0]
FP4_W = [0.5, 0.5, 0.5, 0.5, 1.0, 1.0, 2.0]


def _bc16(t, nb):
    rowlen = t.shape[1]
    return bass.AP(t, 0, [[rowlen, 128], [1, nb], [0, 16]])


def emit_fp4_quant(p, s_v, s_s, src, dst, scale, rinv, vbuf, bbuf, sgn, cols,
                   pre_waits=()):
    nb = cols // 16
    src3 = src.rearrange("p (b k) -> p b k", k=16)
    dst3 = dst.rearrange("p (b k) -> p b k", k=16)
    vb3 = vbuf.rearrange("p (b k) -> p b k", k=16)

    def v1(eng, waits=tuple(pre_waits)):
        W(eng, *waits)
        eng.tensor_reduce(scale[:, 0:nb], src3, AX.X, ALU.max,
                          apply_absolute_value=True)
        eng.tensor_scalar(scale[:, 0:nb], scale[:, 0:nb], 1.0 / 6.0, None, ALU.mult)
        eng.tensor_scalar(scale[:, 0:nb], scale[:, 0:nb], 1e-12, None, ALU.max)
        eng.reciprocal(rinv[:, 0:nb], scale[:, 0:nb])
        eng.tensor_tensor(vb3, src3, _bc16(rinv, nb), ALU.mult).then_inc(s_v.h, 1)
    p.emit("vector", v1)
    v1n = s_v.inc()

    def s1(eng, v1n=v1n):
        W(eng, (s_v.h, v1n))
        eng.activation(sgn, vbuf, AF.Sign)
        eng.activation(vbuf, vbuf, AF.Abs).then_inc(s_s.h, 1)
    p.emit("scalar", s1)
    s1n = s_s.inc()

    def v2(eng, s1n=s1n):
        W(eng, (s_s.h, s1n))
        eng.tensor_scalar(dst, vbuf, FP4_THR[0], None, ALU.is_gt)
        eng.tensor_scalar(dst, dst, FP4_W[0], None, ALU.mult)
        for i in range(1, 7):
            eng.tensor_scalar(bbuf, vbuf, FP4_THR[i], None, ALU.is_gt)
            eng.scalar_tensor_tensor(dst, bbuf, FP4_W[i], dst, ALU.mult, ALU.add)
        eng.tensor_mul(dst, dst, sgn)
        eng.tensor_tensor(dst3, dst3, _bc16(scale, nb), ALU.mult).then_inc(s_v.h, 1)
    p.emit("vector", v2)
    return s_v.inc(), s1n


def build_LGW():
    p = Prog("LGW")
    nc = p.nc
    NTW = 9
    w_d = dram_in(nc, "wtiles", [NTW, 128, 1024])
    q_o = dram_out(nc, "qtiles", [NTW, 128, 1024], F32)

    a = nc.alloc_sbuf_tensor
    wt = [a(f"wt{i}", [128, 1024], F32) for i in range(2)]
    qt_ = [a(f"qt{i}", [128, 1024], F32) for i in range(2)]
    scale = a("scale", [128, 64], F32)
    rinv = a("rinv", [128, 64], F32)
    vbuf = a("vbuf", [128, 1024], F32)
    bbuf = a("bbuf", [128, 1024], F32)
    sgn = a("sgn", [128, 1024], F32)

    qx = DmaQ(p, "qx", 2)
    qo = DmaQ(p, "qo", 2)
    s_v = p.sem("v")
    s_s = p.sem("s")

    out_of = []
    vdone_of = []
    for i in range(NTW):
        wb, qb = wt[i % 2], qt_[i % 2]
        lw = [(s_v.h, vdone_of[i - 2])] if i >= 2 else []
        ldt = p.dma(qx, "sync", wb[:, :], w_d[i, :, :], waits=lw)
        ow = [out_of[i - 2]] if i >= 2 else []
        vdn, _ = emit_fp4_quant(p, s_v, s_s, wb[:, :], qb[:, :], scale, rinv,
                                vbuf[:, :], bbuf[:, :], sgn[:, :], 1024,
                                pre_waits=[ldt, *ow])
        vdone_of.append(vdn)
        out_of.append(p.dma(qo, "sync", q_o[i, :, :], qb[:, :],
                            waits=[(s_v.h, vdn)]))

    def fin(eng, outs=tuple(qo.all_done())):
        W(eng, *outs)
    p.emit("sync", fin)
    return p.finish()



def build_LG():
    p = Prog("LG")
    nc = p.nc
    NT = LOC // 128
    I2816 = 2816
    NC22 = I2816 // 128

    x2_d = dram_in(nc, "x2", [LOC, D_MODEL])
    h6_d = dram_in(nc, "h6", [LOC, D_MODEL])
    wg_d = dram_in(nc, "wgqT", [D_MODEL, I2816])
    wu_d = dram_in(nc, "wuqT", [D_MODEL, I2816])
    wd_d = dram_in(nc, "wdqT", [I2816, D_MODEL])
    ident_d = dram_in(nc, "ident", [128, 128])

    out_o = dram_out(nc, "out", [LOC, D_MODEL], F32)

    a = nc.alloc_sbuf_tensor
    wgt = a("wgt", [128, 8 * I2816], BF16)
    wut = a("wut", [128, 8 * I2816], BF16)
    wdh = [a("wdh0", [128, 11 * D_MODEL], BF16)]
    wdh = [wdh[0], wdh[0]]
    xqT = [a(f"xqT{i}", [128, 8 * 128], BF16) for i in range(2)]
    ident = a("ident_sb", [128, 128], BF16)
    cst = a("cst", [128, 1], F32)
    xt = [a(f"xt{i}", [128, D_MODEL], F32) for i in range(2)]
    outb = [a(f"outb{i}", [128, D_MODEL], F32) for i in range(2)]
    nm = a("nm", [128, 2], F32)
    ssq = a("ssq", [128, 2], F32)
    rstd = a("rstd", [128, 1], F32)
    xq = a("xq", [128, D_MODEL], F32)
    xq_bf = a("xq_bf", [128, D_MODEL], BF16)
    scale = a("scale", [128, 176], F32)
    rinv = a("rinv", [128, 176], F32)
    vbuf = a("vbuf", [128, I2816], F32)
    bbuf = a("bbuf", [128, I2816], BF16)
    sgn = a("sgn", [128, I2816], BF16)
    gate = a("gate", [128, I2816], F32)
    hbuf = a("hbuf", [128, I2816], F32)
    hq_bf = a("hq_bf", [128, I2816], BF16)
    hqT = a("hqT", [128, NC22 * 128], BF16)

    qx = DmaQ(p, "qx", 2)
    qw = DmaQ(p, "qw", 3)
    qd = DmaQ(p, "qd", 2)
    qo = DmaQ(p, "qo", 2)
    s_v = p.sem("v")
    s_s = p.sem("s")
    s_t = p.sem("t")

    g_id = p.dma(qw, "gpsimd", ident[:, :], ident_d)
    for k in range(8):
        p.dma(qw, "gpsimd", wgt[:, k * I2816:(k + 1) * I2816],
              wg_d[k * 128:(k + 1) * 128, :])
    for k in range(8):
        p.dma(qw, "gpsimd", wut[:, k * I2816:(k + 1) * I2816],
              wu_d[k * 128:(k + 1) * 128, :])
    w_all = tuple(qw.all_done())

    def gcst(eng):
        eng.memset(cst[:, 0:1], EPS).then_inc(s_v.h, 1)
    p.emit("vector", gcst)
    cst_done_v = s_v.inc()

    GCH = [(0, 512), (512, 512), (1024, 512), (1536, 512), (2048, 512), (2560, 256)]
    with nc.psum_tensor("ptr", [128, 1024], BF16) as ptr, \
         nc.psum_tensor("pg0", [128, 512], F32) as pg0, \
         nc.psum_tensor("pg1", [128, 512], F32) as pg1, \
         nc.psum_tensor("pd0", [128, 512], F32) as pd0, \
         nc.psum_tensor("pd1", [128, 512], F32) as pd1:
        pg = [pg0, pg1]
        pd = [pd0, pd1]
        out_of = []
        v6_of = []
        wd_of = {}
        pg_ev = {0: (None, 0), 1: (None, 0)}
        for i in range(NT):
            xb = xt[i % 2]
            hb6 = outb[i % 2]
            xT = xqT[i % 2]
            lw = [(s_v.h, v6_of[i - 2])] if i >= 2 else []
            ldx = p.dma(qx, "sync", xb[:, :], x2_d[i * 128:(i + 1) * 128, :], waits=lw)
            hw_ = [out_of[i - 2]] if i >= 2 else []
            ldh = p.dma(qx, "sync", hb6[:, :], h6_d[i * 128:(i + 1) * 128, :], waits=hw_)

            def v1(eng, xb=xb, hb6=hb6, ldx=ldx, ldh=ldh, cd=cst_done_v, sfree=s_s.n):
                W(eng, ldx, ldh, (s_s.h, sfree))
                eng.wait_ge(s_v.h, cd)
                eng.tensor_add(xb[:, :], xb[:, :], hb6[:, :])
                eng.tensor_reduce(nm[:, 0:1], xb[:, :], AX.X, ALU.add, negate=True)
                eng.tensor_scalar(nm[:, 1:2], nm[:, 0:1], 1.0 / D_MODEL, None, ALU.mult)
                eng.tensor_scalar(xq[:, :], xb[:, :], nm[:, 1:2], None, ALU.add
                                  ).then_inc(s_v.h, 1)
            p.emit("vector", v1)
            v1n = s_v.inc()

            def s1(eng, v1n=v1n):
                W(eng, (s_v.h, v1n))
                eng.activation(vbuf[:, 0:D_MODEL], xq[:, :], AF.Square,
                               accum_out=ssq[:, 0:1])
                eng.activation(ssq[:, 1:2], ssq[:, 0:1], AF.Ln, bias=cst[:, 0:1],
                               scale=1.0 / D_MODEL)
                eng.activation(rstd[:, 0:1], ssq[:, 1:2], AF.Exp, scale=-0.5
                               ).then_inc(s_s.h, 1)
            p.emit("scalar", s1)
            s1n = s_s.inc()

            def v2(eng, s1n=s1n):
                W(eng, (s_s.h, s1n))
                eng.tensor_scalar(xq[:, :], xq[:, :], rstd[:, 0:1], None, ALU.mult
                                  ).then_inc(s_v.h, 1)
            p.emit("vector", v2)
            v2n = s_v.inc()

            vqn, _ = emit_fp4_quant(p, s_v, s_s, xq[:, :], xq_bf[:, :], scale, rinv,
                                    vbuf[:, 0:D_MODEL], bbuf[:, 0:D_MODEL],
                                    sgn[:, 0:D_MODEL], D_MODEL,
                                    pre_waits=[(s_v.h, v2n)])

            def t1(eng, vqn=vqn, g_id=g_id, vprev=s_v.n):
                W(eng, (s_v.h, max(vqn, vprev)), g_id)
                for j in range(8):
                    ins = eng.transpose(ptr[:, j * 128:(j + 1) * 128],
                                        xq_bf[:, j * 128:(j + 1) * 128], ident[:, :])
                ins.then_inc(s_t.h, 1)
            p.emit("tensor", t1)
            t1n = s_t.inc()

            def v4(eng, xT=xT, t1n=t1n):
                W(eng, (s_t.h, t1n))
                eng.tensor_copy(xT[:, :], ptr[:, :]).then_inc(s_v.h, 1)
            p.emit("vector", v4)
            v4n = s_v.inc()

            ev_of = []
            for wi, wsb in enumerate([wgt, wut]):
                for ci, (c0, cw) in enumerate(GCH):
                    pb = pg[(wi * 6 + ci) % 2]

                    bank = (wi * 6 + ci) % 2

                    def tg(eng, c0=c0, cw=cw, pb=pb, wsb=wsb, xT=xT, v4n=v4n,
                           wa=w_all, ev=pg_ev[bank]):
                        W(eng, (s_v.h, v4n), *wa)
                        if ev[0] == "s":
                            eng.wait_ge(s_s.h, ev[1])
                        elif ev[0] == "v":
                            eng.wait_ge(s_v.h, ev[1])
                        for k in range(8):
                            ins = eng.matmul(pb[:, 0:cw],
                                             xT[:, k * 128:(k + 1) * 128],
                                             wsb[:, k * I2816 + c0:k * I2816 + c0 + cw],
                                             start=(k == 0), stop=(k == 7))
                        ins.then_inc(s_t.h, 1)
                    p.emit("tensor", tg)
                    tgn = s_t.inc()

                    if wi == 0:
                        def sg(eng, c0=c0, cw=cw, pb=pb, tgn=tgn,
                               vfree=(s_v.n if (i > 0 and ci == 0) else 0)):
                            W(eng, (s_t.h, tgn))
                            if vfree:
                                eng.wait_ge(s_v.h, vfree)
                            eng.activation(gate[:, c0:c0 + cw], pb[:, 0:cw], AF.Silu
                                           ).then_inc(s_s.h, 1)
                        p.emit("scalar", sg)
                        ev_of.append(s_s.inc())
                        pg_ev[bank] = ("s", ev_of[-1])
                    else:
                        def vg(eng, c0=c0, cw=cw, pb=pb, tgn=tgn, evn=ev_of[-1],
                               vfree=(s_v.n if (i > 0 and ci == 0) else 0)):
                            W(eng, (s_t.h, tgn), (s_s.h, evn))
                            if vfree:
                                eng.wait_ge(s_v.h, vfree)
                            eng.tensor_mul(hbuf[:, c0:c0 + cw], gate[:, c0:c0 + cw],
                                           pb[:, 0:cw]).then_inc(s_v.h, 1)
                        p.emit("vector", vg)
                        pg_ev[bank] = ("v", s_v.inc())
            vhn = s_v.n

            vqn2, _ = emit_fp4_quant(p, s_v, s_s, hbuf[:, :], hq_bf[:, :], scale, rinv,
                                     vbuf[:, :], bbuf[:, :], sgn[:, :], I2816,
                                     pre_waits=[(s_v.h, vhn)])

            tr_of = []
            for tb in range(3):
                nblk = 8 if tb < 2 else 6

                def t2(eng, tb=tb, nblk=nblk, vqn2=vqn2, vprev=s_v.n):
                    eng.wait_ge(s_v.h, max(vqn2, vprev))
                    for j in range(nblk):
                        c = tb * 8 + j
                        ins = eng.transpose(ptr[:, j * 128:(j + 1) * 128],
                                            hq_bf[:, c * 128:(c + 1) * 128], ident[:, :])
                    ins.then_inc(s_t.h, 1)
                p.emit("tensor", t2)
                t2n = s_t.inc()

                def v5(eng, tb=tb, nblk=nblk, t2n=t2n):
                    W(eng, (s_t.h, t2n))
                    eng.tensor_copy(hqT[:, tb * 1024:tb * 1024 + nblk * 128],
                                    ptr[:, 0:nblk * 128]).then_inc(s_v.h, 1)
                p.emit("vector", v5)
                tr_of.append(s_v.inc())

            dn_of = []
            for dh in range(2):
                nk0, nkn = (0, 11) if dh == 0 else (11, 11)
                wdb = wdh[dh]
                wdld = p.dma(qd, "gpsimd", wdb[:, :],
                             bass.AP(wd_d.tensor, nk0 * 128 * D_MODEL,
                                     [[D_MODEL, 128], [128 * D_MODEL, nkn], [1, D_MODEL]]),
                             waits=[(s_t.h, s_t.n)])

                for eh in range(2):
                    pb = pd[eh]

                    def td(eng, dh=dh, eh=eh, pb=pb, nk0=nk0, nkn=nkn, wdb=wdb,
                           trn=tr_of[-1], wdld=wdld,
                           evac=(v6_of[-1] if (dh == 0 and v6_of) else 0)):
                        W(eng, (s_v.h, trn), wdld)
                        if evac:
                            eng.wait_ge(s_v.h, evac)
                        for k in range(nkn):
                            ins = eng.matmul(pb[:, :],
                                             hqT[:, (nk0 + k) * 128:(nk0 + k + 1) * 128],
                                             wdb[:, k * D_MODEL + eh * 512:
                                                 k * D_MODEL + eh * 512 + 512],
                                             start=(dh == 0 and k == 0),
                                             stop=(dh == 1 and k == nkn - 1))
                        ins.then_inc(s_t.h, 1)
                    p.emit("tensor", td)
                    dn_of.append(s_t.inc())
                wd_of[dh] = (s_t.h, dn_of[-1])

            def v6(eng, i=i, xb=xb, dnn=dn_of[-1],
                   ofree=(out_of[-2] if len(out_of) >= 2 else None)):
                W(eng, (s_t.h, dnn))
                if ofree:
                    W(eng, ofree)
                ob = outb[i % 2]
                eng.tensor_add(ob[:, 0:512], xb[:, 0:512], pd[0][:, :])
                eng.tensor_add(ob[:, 512:1024], xb[:, 512:1024], pd[1][:, :]
                               ).then_inc(s_v.h, 1)
            p.emit("vector", v6)
            v6_of.append(s_v.inc())

            out_of.append(p.dma(qo, "sync", out_o[i * 128:(i + 1) * 128, :],
                                outb[i % 2][:, :], waits=[(s_v.h, v6_of[-1])]))

    def fin(eng, outs=tuple(qo.all_done())):
        W(eng, *outs)
    p.emit("sync", fin)
    return p.finish()



def _pack_pp(v, nch):
    return np.ascontiguousarray(np.asarray(v, np.float32).reshape(nch, 128).T)


def _make_bands4(ch):
    out = np.zeros((16, 128, 4 * BANDW), dtype=np.float32)
    q0 = ch * LOC
    for ti in range(16):
        qpos = q0 + ti * 128 + np.arange(128)[:, None]
        for s in range(4):
            col0 = s * 2048 + ti * 128 - 256
            kpos = col0 + np.arange(BANDW)[None, :]
            valid = (kpos >= 0) & (kpos < L)
            rel = np.exp(-np.abs(qpos - kpos).astype(np.float32) / 16.0)
            out[ti, :, s * BANDW:(s + 1) * BANDW] = (np.exp(rel) - 1.0) * valid
    return out


def run_LH(x2_shards):
    nc = build_LH()
    in_maps = [dict(x=np.ascontiguousarray(x2_shards[c])) for c in range(NCORES)]
    res = run_bass_kernel_spmd(nc, in_maps, core_ids=list(range(NCORES)))
    return [res.results[c]["h"] for c in range(NCORES)]


_bands_cache = {}


def run_LF_layer(inputs, h_shards, d, nc_lf):
    h_full = [np.concatenate([h_shards[b * 4 + c] for c in range(4)], axis=0)
              for b in range(2)]
    Wq = np.asarray(inputs["L1_attn_Wq"], np.float32)[d]
    Wk = np.asarray(inputs["L1_attn_Wk"], np.float32)[d]
    lnw6 = np.stack([
        np.asarray(inputs["L1_attn_ln1_w"], np.float32)[d],
        np.asarray(inputs["L1_attn_ln1_b"], np.float32)[d],
        np.asarray(inputs["L1_attn_ln2_w"], np.float32)[d],
        np.asarray(inputs["L1_attn_ln2_b"], np.float32)[d],
        np.asarray(inputs["L1_attn_ln3_w"], np.float32)[d],
        np.asarray(inputs["L1_attn_ln3_b"], np.float32)[d],
    ])
    common = dict(
        WqT=np.ascontiguousarray(Wq.T),
        WkT=np.ascontiguousarray(Wk.T),
        bqp=_pack_pp(np.asarray(inputs["L1_attn_bq"], np.float32)[d], 8),
        bkp=_pack_pp(np.asarray(inputs["L1_attn_bk"], np.float32)[d], 8),
        lnw6=np.ascontiguousarray(lnw6),
        W1T=np.ascontiguousarray(np.asarray(inputs["L1_attn_W1"], np.float32)[d].T),
        b1p=_pack_pp(np.asarray(inputs["L1_attn_b1"], np.float32)[d], 32),
        W2n=np.ascontiguousarray(np.asarray(inputs["L1_attn_W2"], np.float32)[d].T),
        b2row=np.ascontiguousarray(np.asarray(inputs["L1_attn_b2"], np.float32)[d][None, :]),
        ident=np.eye(128, dtype=np.float32),
        ones1=np.ones((1, 128), dtype=np.float32),
    )
    bands = _bands_cache.setdefault("b", None) or [_make_bands4(ch)
                                                   for ch in range(4)]
    _bands_cache["b"] = bands
    in_maps = []
    for c in range(NCORES):
        b, ch = divmod(c, 4)
        m = dict(common)
        m["hfull"] = np.ascontiguousarray(h_full[b])
        m["hlocal"] = np.ascontiguousarray(h_shards[c], dtype=np.float32)
        m["band"] = bands[ch]
        in_maps.append(m)
    res = run_bass_kernel_spmd(nc_lf, in_maps, core_ids=list(range(NCORES)))
    return [res.results[c]["hnext"] for c in range(NCORES)]


def run_LGW(inputs):
    wg = np.asarray(inputs["L1_ffn_w_gate"], np.float32)
    wu = np.asarray(inputs["L1_ffn_w_up"], np.float32)
    wd = np.asarray(inputs["L1_ffn_w_down"], np.float32)
    wd_pad = np.zeros((1024, 3072), dtype=np.float32)
    wd_pad[:, :2816] = wd
    tiles = []
    for w in (wg, wu):
        tiles.append(w.reshape(22, 128, 1024))
    tiles.append(wd_pad.reshape(8, 128, 3, 1024).transpose(0, 2, 1, 3).reshape(24, 128, 1024))
    allt = np.concatenate(tiles, axis=0)
    NW = allt.shape[0]
    per = 9
    padded = np.zeros((NCORES * per, 128, 1024), dtype=np.float32)
    padded[:NW] = allt
    nc = build_LGW()
    in_maps = [dict(wtiles=np.ascontiguousarray(padded[c * per:(c + 1) * per]))
               for c in range(NCORES)]
    res = run_bass_kernel_spmd(nc, in_maps, core_ids=list(range(NCORES)))
    q = np.concatenate([res.results[c]["qtiles"] for c in range(NCORES)], axis=0)[:NW]
    wgq = q[0:22].reshape(2816, 1024)
    wuq = q[22:44].reshape(2816, 1024)
    wdq = q[44:68].reshape(8, 3, 128, 1024).transpose(0, 2, 1, 3).reshape(1024, 3072)[:, :2816]
    return wgq, wuq, wdq


def run_LG(inputs, x2_shards, h6_shards):
    wgq, wuq, wdq = run_LGW(inputs)
    common = dict(
        wgqT=np.ascontiguousarray(wgq.T),
        wuqT=np.ascontiguousarray(wuq.T),
        wdqT=np.ascontiguousarray(wdq.T),
        ident=np.eye(128, dtype=np.float32),
    )
    nc = build_LG()
    in_maps = []
    for c in range(NCORES):
        m = dict(common)
        m["x2"] = np.ascontiguousarray(x2_shards[c])
        m["h6"] = np.ascontiguousarray(h6_shards[c])
        in_maps.append(m)
    res = run_bass_kernel_spmd(nc, in_maps, core_ids=list(range(NCORES)))
    return [res.results[c]["out"] for c in range(NCORES)]


def kernel(**inputs):
    la = run_LA(inputs)
    lc, ld = run_LC_LD(inputs, la)
    x1_shards = [ld[c]["x1"] for c in range(NCORES)]
    le = run_LE(inputs, x1_shards)
    x2_shards = [le[c]["x2"] for c in range(NCORES)]
    h_shards = run_LH(x2_shards)
    nc_lf = build_LF()
    for d in range(6):
        h_shards = run_LF_layer(inputs, h_shards, d, nc_lf)
    out_shards = run_LG(inputs, x2_shards, h_shards)
    out = np.stack([
        np.concatenate([out_shards[b * 4 + c] for c in range(4)], axis=0)
        for b in range(2)]).astype(np.float32)
    return out
